# version 11
# baseline (speedup 1.0000x reference)
import numpy as np
import ml_dtypes
import concourse.bass as bass
import concourse.mybir as mybir
from concourse.bass_utils import run_bass_kernel_spmd

F32 = mybir.dt.float32
BF16 = mybir.dt.bfloat16
AF = mybir.ActivationFunctionType
ALU = mybir.AluOpType
AX = mybir.AxisListType
BF = ml_dtypes.bfloat16

D = 2048
NH = 16
HDIM = 128
DKV = 512
DQ = 1536
E = 16
MD = 1408
NC_D = D // 128
NC_KV = DKV // 128
NC_Q = DQ // 128
NC_M = MD // 128
EPS = 1e-5
SM_SCALE = 1.0 / 16.0

ENGS = ("t", "s", "v", "g", "y")


def _chunks(lo, hi, w):
    out = []
    while lo < hi:
        out.append((lo, min(w, hi - lo)))
        lo += w
    return out


class Res:

    def __init__(self, name):
        self.name = name
        self.producer = None
        self.readers = []


class Build:
    def __init__(self, nc):
        self.nc = nc
        self.q = {k: [] for k in ENGS}
        self.sems = {}
        self.pn = {k: 0 for k in ("t", "s", "v")}
        self.psem = {}
        self.waited = {k: {} for k in ENGS}
        self.clean = {k: 0 for k in ENGS}
        self.epoch = {k: 0 for k in ENGS}
        self._sem_ctxs = []
        self.pend = {}
        self.default_reads = ()
        for e in ("t", "s", "v"):
            self.psem[e] = self.sem("p_" + e)

    def sem(self, name):
        cm = self.nc.semaphore(name)
        h = cm.__enter__()
        self._sem_ctxs.append(cm)
        self.sems[name] = h
        return h

    def close_sems(self):
        for cm in self._sem_ctxs:
            cm.__exit__(None, None, None)

    def _dep_wait(self, eng, raw_deps, war_deps):
        need_drain = False
        waits = []
        for (se, sem, val) in raw_deps:
            if se == eng:
                need_drain = True
            else:
                key = id(sem)
                if self.waited[eng].get(key, -1) < val:
                    self.waited[eng][key] = val
                    waits.append((sem, val))
        for (se, sem, val) in war_deps:
            if se == eng:
                need_drain = True
                continue
            key = id(sem)
            if self.waited[eng].get(key, -1) < val:
                self.waited[eng][key] = val
                waits.append((sem, val))
        if need_drain and self.clean[eng] < self.epoch[eng]:
            self.q[eng].append(lambda e: e.drain())
            self.clean[eng] = self.epoch[eng]
        for (sem, val) in waits:
            self.q[eng].append(lambda e, s=sem, v=val: e.wait_ge(s, v))

    def op(self, eng, fn, reads=(), war=(), sets=(), tick=True):
        raw_deps = []
        war_deps = []
        reads = tuple(reads) + getattr(self, "default_reads", ())
        for r in reads:
            if r.producer:
                raw_deps.append(r.producer)
        for w in war:
            war_deps.extend(w.readers)
            if w.producer and not w.readers:
                war_deps.append(w.producer)
        self._dep_wait(eng, raw_deps, war_deps)
        self.epoch[eng] += 1
        if not tick:
            assert not sets
            self.q[eng].append(lambda e, f=fn: f(e))
            self.pend.setdefault(eng, []).extend(reads)
            return
        self.pn[eng] += 1
        val = self.pn[eng]
        sem = self.psem[eng]
        self.q[eng].append(lambda e, f=fn, s=sem: f(e).then_inc(s, 1))
        stamp = (eng, sem, val)
        for w in war:
            w.readers = []
        for s_ in sets:
            s_.producer = stamp
            s_.readers = []
        for r in reads:
            r.readers.append(stamp)
        for r in self.pend.get(eng, ()):
            r.readers.append(stamp)
        self.pend[eng] = []

    def mkbuf(self, name):
        r = Res(name)
        r.dmasem = self.sem("d_" + name)
        r.dmacnt = 0
        return r

    def dma(self, eng, out_ap, in_ap, buf, reads=()):
        raw_deps = []
        for r in reads:
            if r.producer:
                raw_deps.append(r.producer)
        war_deps = list(buf.readers)
        if buf.producer and not buf.readers:
            raw_deps.append(buf.producer)
        self._dep_wait(eng, raw_deps, war_deps)
        self.epoch[eng] += 1
        buf.dmacnt += 16
        cnt = buf.dmacnt
        self.q[eng].append(lambda e, o=out_ap, i=in_ap, s=buf.dmasem:
                           e.dma_start(out=o, in_=i).then_inc(s, 16))
        buf.producer = (eng, buf.dmasem, cnt)
        buf.readers = []

    def stream_write(self, out_ap, in_ap, src_res):
        if not hasattr(src_res, "wsem"):
            src_res.wsem = self.sem("w_" + src_res.name)
            src_res.wcnt = 0
        raw_deps = [src_res.producer] if src_res.producer else []
        self._dep_wait("y", raw_deps, [])
        self.epoch["y"] += 1
        src_res.wcnt += 16
        self.q["y"].append(lambda e, o=out_ap, i=in_ap, s=src_res.wsem:
                           e.dma_start(out=o, in_=i).then_inc(s, 16))
        src_res.readers.append(("y", src_res.wsem, src_res.wcnt))

    def fence(self, stream_resses=()):
        vals = {e: self.pn[e] for e in ("t", "s", "v")}
        wdeps = []
        for r in stream_resses:
            if hasattr(r, "wsem"):
                wdeps.append(("y", r.wsem, r.wcnt))
        for eng in ENGS:
            deps = [(oe, self.psem[oe], vals[oe])
                    for oe in ("t", "s", "v") if oe != eng and vals[oe] > 0]
            deps += [d for d in wdeps if d[0] != eng or eng != "y"]
            if eng == "y":
                deps = [d for d in deps if d[1] is not None]
            self._dep_wait(eng, deps, [])
            if eng in ("t", "s", "v"):
                self.q[eng].append(lambda e: e.drain())
                self.clean[eng] = self.epoch[eng]
                self.epoch[eng] += 1

    def barrier_y(self, res_list):
        for r in res_list:
            if hasattr(r, "wsem"):
                self.q["y"].append(
                    lambda e, s=r.wsem, v=r.wcnt: e.wait_ge(s, v))


def build_program(S):
    HALF = S // 2
    QCS = _chunks(0, S, min(512, S // 2))
    nc = bass.Bass("TRN2", target_bir_lowering=False)

    dt = nc.dram_tensor
    h_t = dt("h_t", [D, S], F32, kind="ExternalInput").ap()
    wdkv = dt("wdkv", [NC_D, NC_KV, 128, 128], F32, kind="ExternalInput").ap()
    wuk = dt("wuk", [NC_KV, NH, 128, 128], F32, kind="ExternalInput").ap()
    wuv = dt("wuv", [NC_KV, NH, 128, 128], F32, kind="ExternalInput").ap()
    wdq = dt("wdq", [NC_D, NC_Q, 128, 128], F32, kind="ExternalInput").ap()
    wuq = dt("wuq", [NC_Q, NH, 128, 128], F32, kind="ExternalInput").ap()
    wkr = dt("wkr", [NC_D, NC_D, 128, 128], F32, kind="ExternalInput").ap()
    wo = dt("wo", [NC_D // 4, NC_D, 4, 128, 128], F32, kind="ExternalInput").ap()
    gw = dt("gw", [NC_D, 128, 16], F32, kind="ExternalInput").ap()
    wg = dt("wg", [E, NC_M, NC_D, 128, 128], F32, kind="ExternalInput").ap()
    wu = dt("wu", [E, NC_M, NC_D, 128, 128], F32, kind="ExternalInput").ap()
    wd = dt("wd", [E, NC_D, NC_M, 128, 128], F32, kind="ExternalInput").ap()
    biasp = dt("biasp", [128, 98], F32, kind="ExternalInput").ap()
    t1d = dt("t1d", [128, S], F32, kind="ExternalInput").ap()
    t2d = dt("t2d", [128, S], F32, kind="ExternalInput").ap()
    cst_ao_f = dt("cst_ao_f", [128, 128], F32, kind="ExternalInput").ap()
    cst_ao_b = dt("cst_ao_b", [128, 128], BF16, kind="ExternalInput").ap()
    cst_id = dt("cst_id", [128, 128], F32, kind="ExternalInput").ap()
    cst_pi = dt("cst_pi", [128, 128], BF16, kind="ExternalInput").ap()
    cst_tri = dt("cst_tri", [128, 128], BF16, kind="ExternalInput").ap()
    cst_o1b = dt("cst_o1b", [1, 128], BF16, kind="ExternalInput").ap()
    cst_sel = dt("cst_sel", [16, 16 * 128], BF16, kind="ExternalInput").ap()
    out_t = dt("out_t", [D, S], F32, kind="ExternalOutput").ap()

    ln1_d = dt("ln1_d", [D, S], BF16, kind="Internal").ap()
    cq_d = dt("cq_d", [DQ, S], BF16, kind="Internal").ap()
    ctx_d = dt("ctx_d", [D, S], BF16, kind="Internal").ap()
    x1b_d = dt("x1b_d", [D, HALF], F32, kind="Internal").ap()

    BC_DKV, BC_UK, BC_UV, BC_DQ, BC_UQ, BC_KR, BC_O = 0, 4, 20, 36, 48, 64, 80

    SB_BASE = 16384 + 1024
    off = [SB_BASE]

    def sb(name, shape, dtv, at=None):
        nbytes = int(np.prod(shape[1:])) * (4 if dtv == F32 else 2)
        if at is None:
            at = off[0]
            off[0] = at + ((nbytes + 63) // 64) * 64
        return nc.alloc_sbuf_tensor_at(name, shape, dtv, offset=at).ap()

    A_t = sb("A_t", [128, S], F32)
    B_t = sb("B_t", [128, S], F32)
    cwT = sb("cwT", [16, S], BF16)
    ao_f = sb("ao_f", [128, 128], F32)
    ao_b = sb("ao_b", [128, 128], BF16)
    idn = sb("idn", [128, 128], F32)
    pi_b = sb("pi_b", [128, 128], BF16)
    tri_b = sb("tri_b", [128, 128], BF16)
    o1b = sb("o1b", [1, 128], BF16)
    selm = sb("selm", [16, 16 * 128], BF16)
    bp = sb("bp", [128, 98], F32)
    nc.const_aps.register(F32, 0.0, bp[:, 96:97])
    nc.const_aps.register(F32, 1.0, bp[:, 97:98])
    gw_b = sb("gw_b", [128, 16 * 16], BF16)
    g_sc = [sb(f"g_sc{i}", [128, 16], F32) for i in range(6)]
    mski = sb("mski", [128, 16], mybir.dt.int32)
    negt = sb("negt", [128, 16], F32)

    REG1 = off[0]
    kR = sb("kR", [128, NC_D * S], BF16, at=REG1)
    x1 = sb("x1", [128, NC_D * HALF], F32, at=REG1)
    T1 = sb("T1", [128, S], F32, at=REG1 + NC_D * S * 2)
    T2 = sb("T2", [128, S], F32, at=REG1 + NC_D * S * 2 + S * 4)
    p4o = REG1 + NC_D * S * 2
    hbuf = [sb(f"hbuf{i}", [128, 512], F32, at=p4o + i * 2048) for i in range(2)]
    cxbuf = [sb(f"cxbuf{i}", [128, 512], BF16,
                at=p4o + (4096 + i * 1024 if i < 2 else 12288 + (i - 2) * 1024))
             for i in range(4)]
    x1st = [sb(f"x1st{i}", [128, 512], F32, at=p4o + 6144 + i * 2048) for i in range(2)]
    xsq = sb("xsq", [128, 512], F32, at=p4o + 10240)
    off[0] = REG1 + max(NC_D * S * 2 + S * 8, NC_D * HALF * 4 + 14336)

    REG2 = off[0]
    ln2 = sb("ln2", [128, NC_D * HALF], BF16, at=REG2)
    gs_t = sb("gs_t", [128, HALF], BF16, at=REG2 + NC_D * HALF * 2)
    us_t = sb("us_t", [128, HALF], BF16, at=REG2 + NC_D * HALF * 2 + HALF * 2)
    cwb = sb("cwb", [128, HALF], BF16, at=REG2 + NC_D * HALF * 2 + HALF * 4)
    cqst = [sb(f"cqst{i}", [128, 512], BF16, at=REG2 + i * 1024) for i in range(2)]
    krt = [sb(f"krt{i}", [128, 512], BF16, at=REG2 + 2048 + i * 1024) for i in range(8)]
    rp1 = sb("rp1", [128, 512], BF16, at=REG2 + 10240)
    rp2 = sb("rp2", [128, 512], BF16, at=REG2 + 11264)
    cqr = [sb(f"cqr{i}", [128, NC_Q * 512], BF16, at=REG2 + 12288 + i * NC_Q * 1024)
           for i in range(2)]
    off[0] = REG2 + max(NC_D * HALF * 2 + HALF * 6, 12288 + 2 * NC_Q * 1024)

    REG3 = off[0]
    cKV = sb("cKV", [128, NC_KV * S], BF16, at=REG3)
    t_e = sb("t_e", [128, NC_M * HALF], BF16, at=REG3)
    st_sc = [sb(f"st_sc{i}", [128, 512], F32, at=REG3 + i * 2048) for i in range(4)]
    off[0] = REG3 + max(NC_KV * S * 2 + 2048, NC_M * HALF * 2, 4 * 2048)

    REG4 = off[0]
    xb = [sb(f"xb{i}", [128, 1024], F32, at=REG4 + i * 4096) for i in range(2)]
    lnb = [sb(f"lnb{i}", [128, 1024], BF16, at=REG4 + 8192 + i * 2048) for i in range(2)]
    tmpf = sb("tmpf", [128, 1024], F32, at=REG4 + 12288)
    qC_h = sb("qC_h", [128, S], BF16, at=REG4)
    qR_h = sb("qR_h", [128, S], BF16, at=REG4 + S * 2)
    kC_h = sb("kC_h", [128, S], BF16, at=REG4 + S * 4)
    v_h = sb("v_h", [128, S], BF16, at=REG4 + S * 6)
    expb = [sb(f"expb{i}", [128, 512], BF16, at=REG4 + S * 8 + i * 1024) for i in range(4)]
    cxst = [sb(f"cxst{i}", [128, 512], BF16, at=REG4 + S * 8 + 4096 + i * 1024)
            for i in range(2)]
    rcp = sb("rcp", [128, 512], F32, at=REG3 + NC_KV * S * 2)
    wdb = [sb(f"wdb{i}", [128, NC_M * 128], BF16, at=REG4 + i * NC_M * 256) for i in range(2)]
    off[0] = REG4 + max(16384, S * 8 + 6144, 2 * NC_M * 256)

    REG5 = off[0]
    wb_p2 = [sb(f"wb_p2{i}", [128, 6 * 128], BF16, at=REG5 + i * 1536) for i in range(4)]
    lrhs = [sb(f"lrhs{i}", [128, 512], BF16, at=REG5 + 6144 + i * 1024) for i in range(4)]
    wuq_b = [sb(f"wuq_b{i}", [128, NC_Q * 128], BF16, at=REG5 + i * NC_Q * 256)
             for i in range(2)]
    wukv_b = [sb(f"wukv_b{i}", [128, 2 * NC_KV * 128], BF16,
                 at=REG5 + 2 * NC_Q * 256 + i * 2 * NC_KV * 256) for i in range(2)]
    wo_b = [sb(f"wo_b{i}", [128, 8 * 4 * 128], BF16, at=REG5 + i * 8 * 4 * 256)
            for i in range(2)]
    wgb = [sb(f"wgb{i}", [128, NC_D * 128], BF16, at=REG5 + i * NC_D * 256) for i in range(2)]
    wub = [sb(f"wub{i}", [128, NC_D * 128], BF16, at=REG5 + 2 * NC_D * 256 + i * NC_D * 256)
           for i in range(2)]
    off[0] = REG5 + max(5120, 2 * NC_Q * 256 + 4 * NC_KV * 256, 2 * 8 * 4 * 256,
                        4 * NC_D * 256)
    assert off[0] <= 229376, f"SBUF over budget: {off[0]} B/partition"

    psA = nc.alloc_psum_tensor("psA", [128, 1024], F32).ap()
    psB = nc.alloc_psum_tensor("psB", [128, 1024], F32).ap()
    psC = nc.alloc_psum_tensor("psC", [128, 1024], F32).ap()
    psD = nc.alloc_psum_tensor("psD", [128, 512], F32).ap()
    psE = nc.alloc_psum_tensor("psE", [128, 512], F32).ap()

    b = Build(nc)
    R = lambda n: Res(n)
    r_psA = [R("psA0"), R("psA1")]
    r_psB = [R("psB0"), R("psB1")]
    r_psC = [R("psC0"), R("psC1")]
    r_psD, r_psE = R("psD"), R("psE")
    r_A, r_B = R("A"), R("B")
    r_cKV, r_kR = R("cKV"), R("kR")
    r_kC, r_v = R("kC"), R("v")
    r_x1, r_ln2, r_te, r_cwT, r_cwb = R("x1"), R("ln2"), R("te"), R("cwT"), R("cwb")
    r_gs, r_us = R("gs"), R("us")
    r_expb = [R(f"expb{i}") for i in range(4)]
    r_rcp = R("rcp")
    r_cxst = [R("cxst0"), R("cxst1")]
    r_krt = [R(f"krt{i}") for i in range(8)]
    r_rp1, r_rp2 = R("rp1"), R("rp2")
    r_cqst = [R("cqst0"), R("cqst1")]
    r_x1st = [R("x1st0"), R("x1st1")]
    r_xsq = R("xsq")
    r_stsc = [R(f"stsc{i}") for i in range(4)]
    r_gsc = [R(f"gsc{i}") for i in range(6)]
    r_tmpf = R("tmpf")
    r_lnb = [R("lnb0"), R("lnb1")]
    r_cst = R("consts")

    d_xb = [b.mkbuf("xb0"), b.mkbuf("xb1")]
    d_lrhs = [b.mkbuf(f"lrhs{i}") for i in range(4)]
    d_wbp2 = [b.mkbuf(f"wbp2{i}") for i in range(4)]
    d_cqr = [b.mkbuf("cqr0"), b.mkbuf("cqr1")]
    d_wuq = [b.mkbuf("wuq0"), b.mkbuf("wuq1")]
    d_wukv = [b.mkbuf("wukv0"), b.mkbuf("wukv1")]
    d_wo = [b.mkbuf("wo0"), b.mkbuf("wo1")]
    d_hb = [b.mkbuf("hb0"), b.mkbuf("hb1")]
    d_cx = [b.mkbuf(f"cx{i}") for i in range(4)]
    d_wgb = [b.mkbuf("wgb0"), b.mkbuf("wgb1")]
    d_wub = [b.mkbuf("wub0"), b.mkbuf("wub1")]
    d_wdb = [b.mkbuf("wdb0"), b.mkbuf("wdb1")]
    d_cst = b.mkbuf("cst")
    d_gwc = b.mkbuf("gwc")
    d_x1r = [b.mkbuf("x1r0"), b.mkbuf("x1r1")]


    for dst, src in [(bp, biasp), (ao_f, cst_ao_f), (ao_b, cst_ao_b), (idn, cst_id),
                     (pi_b, cst_pi), (tri_b, cst_tri), (o1b, cst_o1b),
                     (selm, cst_sel), (T1, t1d), (T2, t2d)]:
        b.dma("y", dst[:, :], src[:, :], d_cst)
    for c in range(NC_D):
        b.dma("g", gw_b[:, c * 16:(c + 1) * 16], gw[c, :, :], d_gwc)
    b.op("v", lambda e: e.memset(negt[:, :], -1e30), sets=(r_cst,))
    b.default_reads = (d_cst,)

    def stats_chain(sl, slen, ps_sum_ap, r_sum, ps_sq_ap, r_sq):
        mt, e2, ve, sc3 = st_sc
        m0, m1, m2, m3 = r_stsc
        SL = slice(sl, sl + slen)
        L = slice(0, slen)
        b.op("v", lambda e: e.tensor_scalar(mt[:, L], ps_sum_ap[:, L], 1.0 / D, None,
                                            ALU.mult),
             reads=(r_sum,), war=(m0,), sets=(m0,))
        b.op("v", lambda e: e.tensor_scalar(e2[:, L], ps_sq_ap[:, L], 1.0 / D, None,
                                            ALU.mult),
             reads=(r_sq,), war=(m1,), sets=(m1,))
        b.op("v", lambda e: e.scalar_tensor_tensor(ve[:, L], mt[:, L], -1.0, mt[:, L],
                                                   ALU.mult, ALU.mult),
             reads=(m0,), war=(m2,), sets=(m2,))
        b.op("v", lambda e: e.tensor_tensor(ve[:, L], ve[:, L], e2[:, L], ALU.add),
             reads=(m2, m1), sets=(m2,))
        b.op("v", lambda e: e.tensor_scalar(ve[:, L], ve[:, L], EPS, None, ALU.add),
             reads=(m2,), sets=(m2,))
        b.op("s", lambda e: e.activation(sc3[:, L], ve[:, L], AF.Sqrt),
             reads=(m2,), war=(m3,), sets=(m3,))
        b.op("v", lambda e: e.reciprocal(sc3[:, L], sc3[:, L]),
             reads=(m3,), sets=(m3,))
        b.op("v", lambda e: e.tensor_tensor(A_t[:, SL], sc3[:, L], sc3[:, L], ALU.mult),
             reads=(m3,), war=(r_A,), sets=(r_A,))
        b.op("v", lambda e: e.tensor_tensor(A_t[:, SL], A_t[:, SL], ve[:, L], ALU.mult),
             reads=(r_A, m2), sets=(r_A,))
        b.op("v", lambda e: e.tensor_scalar(A_t[:, SL], A_t[:, SL], -0.5, 1.5,
                                            ALU.mult, ALU.add),
             reads=(r_A,), sets=(r_A,))
        b.op("v", lambda e: e.tensor_tensor(A_t[:, SL], A_t[:, SL], sc3[:, L], ALU.mult),
             reads=(r_A, m3), sets=(r_A,))
        b.op("v", lambda e: e.scalar_tensor_tensor(B_t[:, SL], mt[:, L], -1.0, A_t[:, SL],
                                                   ALU.mult, ALU.mult),
             reads=(m0, r_A), war=(r_B,), sets=(r_B,))

    for (qo, ql) in _chunks(0, S, 512):
        for c in range(NC_D):
            i = c % 2
            b.dma("y", xb[i][:, 0:ql], h_t[c * 128:(c + 1) * 128, qo:qo + ql], d_xb[i])
            b.op("s", lambda e, i=i, ql=ql: e.activation(st_sc[3][:, 0:ql],
                                                         xb[i][:, 0:ql], AF.Square),
                 reads=(d_xb[i],), war=(r_stsc[3],), sets=(r_stsc[3],))
            b.op("t", lambda e, i=i, ql=ql, c=c: e.matmul(
                psD[:, 0:ql], ao_f[:, :], xb[i][:, 0:ql],
                start=(c == 0), stop=(c == NC_D - 1)),
                reads=(d_xb[i], d_cst),
                war=((r_psD,) if c == 0 else ()),
                sets=((r_psD,) if c == NC_D - 1 else ()), tick=(c == NC_D - 1))
            b.op("t", lambda e, ql=ql, c=c: e.matmul(
                psE[:, 0:ql], ao_f[:, :], st_sc[3][:, 0:ql],
                start=(c == 0), stop=(c == NC_D - 1)),
                reads=(r_stsc[3],),
                war=((r_psE,) if c == 0 else ()),
                sets=((r_psE,) if c == NC_D - 1 else ()), tick=True)
        stats_chain(qo, ql, psD, r_psD, psE, r_psE)

    for c in range(NC_D):
        for (ho, hl) in _chunks(0, S, 1024):
            i = (c * (S // 1024) + ho // 1024) % 2
            b.dma("y", xb[i][:, 0:hl], h_t[c * 128:(c + 1) * 128, ho:ho + hl], d_xb[i])
            b.op("v", lambda e, i=i, ho=ho, hl=hl: e.tensor_tensor(
                tmpf[:, 0:hl], xb[i][:, 0:hl], A_t[:, ho:ho + hl], ALU.mult),
                reads=(d_xb[i], r_A), war=(r_tmpf,), sets=(r_tmpf,))
            b.op("v", lambda e, i=i, ho=ho, hl=hl: e.tensor_tensor(
                lnb[i][:, 0:hl], tmpf[:, 0:hl], B_t[:, ho:ho + hl], ALU.add),
                reads=(r_tmpf, r_B), war=(r_lnb[i],), sets=(r_lnb[i],))
            b.stream_write(ln1_d[c * 128:(c + 1) * 128, ho:ho + hl], lnb[i][:, 0:hl],
                           r_lnb[i])
    b.barrier_y(r_lnb)
    b.fence(stream_resses=(r_lnb,))

    def rope_combine(dst_ap, r_dst, src_ap, r_src, qo, ql):
        b.op("v", lambda e: e.tensor_tensor(rp1[:, 0:ql], src_ap, T1[:, qo:qo + ql],
                                            ALU.mult),
             reads=(r_src, d_cst), war=(r_rp1,), sets=(r_rp1,))
        b.op("v", lambda e: e.tensor_tensor(rp2[:, 0:ql], psD[:, 0:ql], T2[:, qo:qo + ql],
                                            ALU.mult),
             reads=(r_psD, d_cst), war=(r_rp2,), sets=(r_rp2,))
        b.op("v", lambda e: e.tensor_tensor(dst_ap, rp1[:, 0:ql], rp2[:, 0:ql], ALU.add),
             reads=(r_rp1, r_rp2), sets=(r_dst,))

    passes = [("dkv", wdkv, 0, NC_KV, BC_DKV)]
    for g0 in range(0, NC_D, 6):
        passes.append(("kr", wkr, g0, min(g0 + 6, NC_D), BC_KR))
    for g0 in range(0, NC_Q, 6):
        passes.append(("dq", wdq, g0, min(g0 + 6, NC_Q), BC_DQ))
    pshalf = [(psA, 0, r_psA[0]), (psA, 512, r_psA[1]), (psB, 0, r_psB[0]),
              (psB, 512, r_psB[1]), (psC, 0, r_psC[0]), (psC, 512, r_psC[1])]
    ibw = [0]
    pending_rope = []
    rope_alt = [0]
    krt_rr = [0]
    for kind, wt, m0g, m1g, bc0 in passes:
        nm = m1g - m0g
        for (qo, ql) in QCS:
            for c in range(NC_D):
                if c == 4:
                    for job in pending_rope:
                        job()
                    pending_rope.clear()
                iw = ibw[0] % 4
                ibw[0] += 1
                b.dma("g", wb_p2[iw][:, 0:nm * 128].rearrange("p (m f) -> p m f", m=nm),
                      wt[c, m0g:m1g, :, :].rearrange("m p f -> p m f"), d_wbp2[iw])
                b.dma("y", lrhs[iw][:, 0:ql], ln1_d[c * 128:(c + 1) * 128, qo:qo + ql],
                      d_lrhs[iw])
                for m in range(nm):
                    ps, po, rps = pshalf[m]
                    b.op("t", lambda e, ps=ps, po=po, ql=ql, iw=iw, m=m, c=c: e.matmul(
                        ps[:, po:po + ql], wb_p2[iw][:, m * 128:(m + 1) * 128],
                        lrhs[iw][:, 0:ql], start=(c == 0), stop=(c == NC_D - 1)),
                        reads=(d_wbp2[iw], d_lrhs[iw]),
                        war=((rps,) if c == 0 else ()),
                        sets=((rps,) if c == NC_D - 1 else ()),
                        tick=(m == nm - 1 or c == NC_D - 1))
            for m in range(nm):
                ps, po, rps = pshalf[m]
                ma = m0g + m
                if kind == "dkv":
                    b.op("s", lambda e, ps=ps, po=po, ql=ql, ma=ma, qo=qo: e.activation(
                        cKV[:, ma * S + qo:ma * S + qo + ql], ps[:, po:po + ql],
                        AF.Identity, bias=bp[:, BC_DKV + ma:BC_DKV + ma + 1]),
                        reads=(rps, d_cst), war=(r_cKV,), sets=(r_cKV,))
                elif kind == "dq":
                    i = m % 2
                    b.op("s", lambda e, ps=ps, po=po, ql=ql, ma=ma, i=i: e.activation(
                        cqst[i][:, 0:ql], ps[:, po:po + ql], AF.Identity,
                        bias=bp[:, BC_DQ + ma:BC_DQ + ma + 1]),
                        reads=(rps, d_cst), war=(r_cqst[i],), sets=(r_cqst[i],))
                    b.stream_write(cq_d[ma * 128:(ma + 1) * 128, qo:qo + ql],
                                   cqst[i][:, 0:ql], r_cqst[i])
                else:
                    i = krt_rr[0] % 8
                    krt_rr[0] += 1
                    b.op("s", lambda e, ps=ps, po=po, ql=ql, ma=ma, i=i: e.activation(
                        krt[i][:, 0:ql], ps[:, po:po + ql], AF.Identity,
                        bias=bp[:, BC_KR + ma:BC_KR + ma + 1]),
                        reads=(rps, d_cst), war=(r_krt[i],), sets=(r_krt[i],))

                    def rope_job(ma=ma, qo=qo, ql=ql, i=i):
                        pb, r_pb = ([psD, psE], [r_psD, r_psE])[0], None
                        ib = rope_alt[0] % 2
                        rope_alt[0] += 1
                        pbank = psD if ib == 0 else psE
                        r_pbank = r_psD if ib == 0 else r_psE
                        b.op("t", lambda e: e.matmul(
                            pbank[:, 0:ql], pi_b[:, :], krt[i][:, 0:ql],
                            start=True, stop=True),
                            reads=(r_krt[i], d_cst), war=(r_pbank,), sets=(r_pbank,))
                        b.op("v", lambda e: e.tensor_tensor(
                            rp1[:, 0:ql], krt[i][:, 0:ql], T1[:, qo:qo + ql], ALU.mult),
                            reads=(r_krt[i], d_cst), war=(r_rp1,), sets=(r_rp1,))
                        b.op("v", lambda e: e.tensor_tensor(
                            rp2[:, 0:ql], pbank[:, 0:ql], T2[:, qo:qo + ql], ALU.mult),
                            reads=(r_pbank, d_cst), war=(r_rp2,), sets=(r_rp2,))
                        b.op("v", lambda e: e.tensor_tensor(
                            kR[:, ma * S + qo:ma * S + qo + ql], rp1[:, 0:ql],
                            rp2[:, 0:ql], ALU.add),
                            reads=(r_rp1, r_rp2), sets=(r_kR,))
                    pending_rope.append(rope_job)
    for job in pending_rope:
        job()
    pending_rope.clear()
    b.barrier_y(r_cqst)
    b.fence(stream_resses=(r_cqst[0], r_cqst[1]))

    NQ = len(QCS)
    r_qCq = [R(f"qC{q}") for q in range(NQ)]
    r_qRq = [R(f"qR{q}") for q in range(NQ)]

    def rope_combine_e(dst_ap, r_dst, src_ap, r_src, qo, ql):
        b.op("v", lambda e: e.tensor_tensor(rp1[:, 0:ql], src_ap, T1[:, qo:qo + ql],
                                            ALU.mult),
             reads=(r_src, d_cst), war=(r_rp1,), sets=(r_rp1,))
        b.op("v", lambda e: e.tensor_tensor(rp2[:, 0:ql], psE[:, 0:ql], T2[:, qo:qo + ql],
                                            ALU.mult),
             reads=(r_psE, d_cst), war=(r_rp2,), sets=(r_rp2,))
        b.op("v", lambda e: e.tensor_tensor(dst_ap, rp1[:, 0:ql], rp2[:, 0:ql], ALU.add),
             reads=(r_rp1, r_rp2), sets=(r_dst,))

    def prod_dmas(h):
        iw = h % 2
        b.dma("g", wuq_b[iw][:, :].rearrange("p (k f) -> p k f", k=NC_Q),
              wuq[:, h, :, :].rearrange("k p f -> p k f"), d_wuq[iw])
        b.dma("g", wukv_b[iw][:, 0:NC_KV * 128].rearrange("p (k f) -> p k f", k=NC_KV),
              wuk[:, h, :, :].rearrange("k p f -> p k f"), d_wukv[iw])
        b.dma("g", wukv_b[iw][:, NC_KV * 128:].rearrange("p (k f) -> p k f", k=NC_KV),
              wuv[:, h, :, :].rearrange("k p f -> p k f"), d_wukv[iw])

    def prefetch_cqr(qi):
        qo, ql = QCS[qi]
        ic = qi % 2
        b.dma("y", cqr[ic][:, 0:NC_Q * ql].rearrange("p (k f) -> p k f", k=NC_Q),
              cq_d.rearrange("(k p) t -> p k t", p=128)[:, :, qo:qo + ql], d_cqr[ic])

    def prod_qc(h, qi):
        iw = h % 2
        qo, ql = QCS[qi]
        ic = qi % 2
        for kc in range(NC_Q):
            b.op("t", lambda e, kc=kc, ql=ql, ic=ic, iw=iw: e.matmul(
                psE[:, 0:ql], wuq_b[iw][:, kc * 128:(kc + 1) * 128],
                cqr[ic][:, kc * ql:kc * ql + ql],
                start=(kc == 0), stop=(kc == NC_Q - 1)),
                reads=(d_wuq[iw], d_cqr[ic]),
                war=((r_psE,) if kc == 0 else ()),
                sets=((r_psE,) if kc == NC_Q - 1 else ()), tick=(kc == NC_Q - 1))
        b.op("s", lambda e, qo=qo, ql=ql, h=h: e.activation(
            qC_h[:, qo:qo + ql], psE[:, 0:ql], AF.Identity,
            bias=bp[:, BC_UQ + h:BC_UQ + h + 1]),
            reads=(r_psE, d_cst), war=(r_qCq[qi],), sets=(r_qCq[qi],))
        b.op("t", lambda e, qo=qo, ql=ql, qi=qi: e.matmul(
            psE[:, 0:ql], pi_b[:, :], qC_h[:, qo:qo + ql], start=True, stop=True),
            reads=(r_qCq[qi], d_cst), war=(r_psE,), sets=(r_psE,))
        rope_combine_e(qR_h[:, qo:qo + ql], r_qRq[qi], qC_h[:, qo:qo + ql],
                       r_qCq[qi], qo, ql)

    def prod_kv(h):
        iw = h % 2
        for (qo, ql) in QCS:
            for kc in range(NC_KV):
                b.op("t", lambda e, kc=kc, qo=qo, ql=ql, iw=iw: e.matmul(
                    psE[:, 0:ql], wukv_b[iw][:, kc * 128:(kc + 1) * 128],
                    cKV[:, kc * S + qo:kc * S + qo + ql],
                    start=(kc == 0), stop=(kc == NC_KV - 1)),
                    reads=(d_wukv[iw], r_cKV),
                    war=((r_psE,) if kc == 0 else ()),
                    sets=((r_psE,) if kc == NC_KV - 1 else ()), tick=(kc == NC_KV - 1))
            b.op("s", lambda e, qo=qo, ql=ql, h=h: e.activation(
                kC_h[:, qo:qo + ql], psE[:, 0:ql], AF.Identity,
                bias=bp[:, BC_UK + h:BC_UK + h + 1]),
                reads=(r_psE, d_cst), war=(r_kC,), sets=(r_kC,))
        ntt = S // 128
        for g0 in range(0, ntt, 4):
            gn = min(4, ntt - g0)
            for tj in range(gn):
                tt = g0 + tj
                for kc in range(NC_KV):
                    b.op("t", lambda e, tt=tt, tj=tj, kc=kc, iw=iw: e.matmul(
                        psE[:, tj * 128:(tj + 1) * 128],
                        cKV[:, kc * S + tt * 128:kc * S + (tt + 1) * 128],
                        wukv_b[iw][:, (NC_KV + kc) * 128:(NC_KV + kc + 1) * 128],
                        start=(kc == 0), stop=(kc == NC_KV - 1)),
                        reads=(d_wukv[iw], r_cKV),
                        war=((r_psE,) if (kc == 0 and tj == 0) else ()),
                        sets=((r_psE,) if (kc == NC_KV - 1 and tj == gn - 1) else ()),
                        tick=(kc == NC_KV - 1 and tj == gn - 1))
            b.op("s", lambda e, g0=g0, gn=gn: e.activation(
                v_h[:, g0 * 128:(g0 + gn) * 128], psE[:, 0:gn * 128], AF.Identity),
                reads=(r_psE,), war=(r_v,), sets=(r_v,))

    sslot = [(psC, 0, r_psC[0]), (psC, 512, r_psC[1]),
             (psB, 0, r_psB[0]), (psB, 512, r_psB[1])]

    def attn_qc(h, qi):
        qo, ql = QCS[qi]
        nkb = (qo + ql - 1) // 128 + 1
        LAG = min(2, nkb)
        dnb, r_dnb = (psD, r_psD) if qi % 2 == 0 else (psA[:, 512:1024], r_psA[1])

        def emit_scores(kb):
            ie = kb % 4
            ps_s, pco, rsc = sslot[ie]
            b.op("t", lambda e, kb=kb, qo=qo, ql=ql, ps_s=ps_s, pco=pco, qi=qi: e.matmul(
                ps_s[:, pco:pco + ql], kC_h[:, kb * 128:(kb + 1) * 128],
                qC_h[:, qo:qo + ql], start=True, stop=False),
                reads=(r_kC, r_qCq[qi]), war=(rsc,), tick=False)
            b.op("t", lambda e, kb=kb, qo=qo, ql=ql, ps_s=ps_s, pco=pco, h=h, qi=qi:
                 e.matmul(ps_s[:, pco:pco + ql],
                          kR[:, h * S + kb * 128:h * S + (kb + 1) * 128],
                          qR_h[:, qo:qo + ql], start=False, stop=True),
                 reads=(r_kR, r_qRq[qi]), sets=(rsc,))
            b.op("s", lambda e, ps_s=ps_s, pco=pco, ql=ql, ie=ie: e.activation(
                expb[ie][:, 0:ql], ps_s[:, pco:pco + ql], AF.Exp, scale=SM_SCALE),
                reads=(rsc,), war=(r_expb[ie],), sets=(r_expb[ie],))
            dcol = kb * 128 - qo
            if dcol >= 0:
                if dcol > 0:
                    b.op("v", lambda e, ie=ie, dcol=dcol: e.memset(
                        expb[ie][:, 0:dcol], 0.0),
                        reads=(r_expb[ie],), sets=(r_expb[ie],))
                hi = min(dcol + 128, ql)
                b.op("v", lambda e, ie=ie, dcol=dcol, hi=hi: e.tensor_tensor(
                    expb[ie][:, dcol:hi], expb[ie][:, dcol:hi],
                    tri_b[:, 0:hi - dcol], ALU.mult),
                    reads=(r_expb[ie], d_cst), sets=(r_expb[ie],))

        def emit_consume(kb):
            ie = kb % 4
            b.op("t", lambda e, ie=ie, ql=ql, kb=kb, nkb=nkb: e.matmul(
                dnb[:, 0:ql], ao_b[:, :], expb[ie][:, 0:ql],
                start=(kb == 0), stop=(kb == nkb - 1)),
                reads=(r_expb[ie], d_cst),
                war=((r_dnb,) if kb == 0 else ()),
                sets=((r_dnb,) if kb == nkb - 1 else ()), tick=(kb == nkb - 1))
            b.op("t", lambda e, ie=ie, ql=ql, kb=kb, nkb=nkb: e.matmul(
                psA[:, 0:ql], v_h[:, kb * 128:(kb + 1) * 128], expb[ie][:, 0:ql],
                start=(kb == 0), stop=(kb == nkb - 1)),
                reads=(r_v, r_expb[ie]),
                war=((r_psA[0],) if kb == 0 else ()),
                sets=((r_psA[0],) if kb == nkb - 1 else ()), tick=True)

        for kb in range(nkb + LAG):
            if kb < nkb:
                emit_scores(kb)
            if kb >= LAG:
                emit_consume(kb - LAG)
        ix = qi % 2
        b.op("v", lambda e, ql=ql: e.reciprocal(rcp[:, 0:ql], dnb[:, 0:ql]),
             reads=(r_dnb,), war=(r_rcp,), sets=(r_rcp,))
        b.op("v", lambda e, ql=ql, ix=ix: e.tensor_tensor(
            cxst[ix][:, 0:ql], psA[:, 0:ql], rcp[:, 0:ql], ALU.mult),
            reads=(r_psA[0], r_rcp), war=(r_cxst[ix],), sets=(r_cxst[ix],))
        b.op("v", lambda e, ql=ql, ix=ix, h=h: e.tensor_scalar(
            cxst[ix][:, 0:ql], cxst[ix][:, 0:ql],
            bp[:, BC_UV + h:BC_UV + h + 1], None, ALU.add),
            reads=(r_cxst[ix], d_cst), sets=(r_cxst[ix],))
        b.stream_write(ctx_d[h * 128:(h + 1) * 128, qo:qo + ql],
                       cxst[ix][:, 0:ql], r_cxst[ix])

    prod_dmas(0)
    for qi in range(NQ):
        prefetch_cqr(qi)
        prod_qc(0, qi)
    prod_kv(0)
    for h in range(NH):
        if h + 1 < NH:
            prod_dmas(h + 1)
        for qi in range(NQ):
            if h + 1 < NH:
                prefetch_cqr(qi)
            attn_qc(h, qi)
            if h + 1 < NH:
                prod_qc(h + 1, qi)
        if h + 1 < NH:
            prod_kv(h + 1)
    b.barrier_y(r_cxst)
    b.fence(stream_resses=(r_cxst[0], r_cxst[1]))

    pj = [(psA, 0, r_psA[0]), (psA, 512, r_psA[1]), (psB, 0, r_psB[0]),
          (psB, 512, r_psB[1])]
    for (qo, ql) in QCS:
        cidx = [0]
        for jg in range(NC_D // 4):
            for hh in range(2):
                b.dma("g", wo_b[hh][:, :].rearrange("p (kj f) -> p kj f", kj=32),
                      wo[jg, hh * 8:(hh + 1) * 8, :, :, :]
                      .rearrange("k j p f -> p (k j) f"), d_wo[hh])
            for hd in range(NC_D):
                icx = hd % 4
                b.dma("y", cxbuf[icx][:, 0:ql],
                      ctx_d[hd * 128:(hd + 1) * 128, qo:qo + ql], d_cx[icx])
                hh, hdo = hd // 8, hd % 8
                for j in range(4):
                    ps, po, rps = pj[j]
                    b.op("t", lambda e, ps=ps, po=po, ql=ql, hh=hh, hdo=hdo, icx=icx,
                         j=j, hd=hd: e.matmul(
                             ps[:, po:po + ql],
                             wo_b[hh][:, (hdo * 4 + j) * 128:(hdo * 4 + j + 1) * 128],
                             cxbuf[icx][:, 0:ql],
                             start=(hd == 0), stop=(hd == NC_D - 1)),
                        reads=(d_wo[hh], d_cx[icx]),
                        war=((rps,) if hd == 0 else ()),
                        sets=((rps,) if hd == NC_D - 1 else ()),
                        tick=(j == 3 or hd == NC_D - 1))
            for j in range(4):
                ps, po, rps = pj[j]
                jj = jg * 4 + j
                ih = jj % 2
                b.dma("y", hbuf[ih][:, 0:ql], h_t[jj * 128:(jj + 1) * 128, qo:qo + ql],
                      d_hb[ih])
                if qo < HALF:
                    dst = x1[:, jj * HALF + qo:jj * HALF + qo + ql]
                    rdst = r_x1
                else:
                    ixs = jj % 2
                    dst = x1st[ixs][:, 0:ql]
                    rdst = r_x1st[ixs]
                b.op("v", lambda e, ps=ps, po=po, ql=ql, jj=jj, ih=ih, dst=dst:
                     e.scalar_tensor_tensor(dst, ps[:, po:po + ql],
                                            bp[:, BC_O + jj:BC_O + jj + 1],
                                            hbuf[ih][:, 0:ql], ALU.add, ALU.add),
                     reads=(rps, d_hb[ih], d_cst), war=(rdst,), sets=(rdst,))
                if qo >= HALF:
                    ixs = jj % 2
                    b.stream_write(
                        x1b_d[jj * 128:(jj + 1) * 128, qo - HALF:qo - HALF + ql],
                        x1st[ixs][:, 0:ql], r_x1st[ixs])
                    sfs, r_sfs = x1st[ixs][:, 0:ql], r_x1st[ixs]
                else:
                    sfs, r_sfs = dst, r_x1
                ci = cidx[0]
                cidx[0] += 1
                b.op("s", lambda e, sfs=sfs, ql=ql: e.activation(
                    xsq[:, 0:ql], sfs, AF.Square),
                    reads=(r_sfs,), war=(r_xsq,), sets=(r_xsq,))
                b.op("t", lambda e, sfs=sfs, ql=ql, ci=ci: e.matmul(
                    psC[:, 0:ql], ao_f[:, :], sfs,
                    start=(ci == 0), stop=(ci == NC_D - 1)),
                    reads=(r_sfs, d_cst),
                    war=((r_psC[0],) if ci == 0 else ()),
                    sets=((r_psC[0],) if ci == NC_D - 1 else ()), tick=(ci == NC_D - 1))
                b.op("t", lambda e, ql=ql, ci=ci: e.matmul(
                    psC[:, 512:512 + ql], ao_f[:, :], xsq[:, 0:ql],
                    start=(ci == 0), stop=(ci == NC_D - 1)),
                    reads=(r_xsq,),
                    war=((r_psC[1],) if ci == 0 else ()),
                    sets=((r_psC[1],) if ci == NC_D - 1 else ()), tick=True)
        stats_chain(qo, ql, psC, r_psC[0], psC[:, 512:1024], r_psC[1])

    def ln2_gate(t0):
        for c in range(NC_D):
            for (ho, hl) in _chunks(0, HALF, 1024):
                b.op("v", lambda e, c=c, ho=ho, hl=hl: e.tensor_tensor(
                    tmpf[:, 0:hl], x1[:, c * HALF + ho:c * HALF + ho + hl],
                    A_t[:, t0 + ho:t0 + ho + hl], ALU.mult),
                    reads=(r_x1, r_A), war=(r_tmpf,), sets=(r_tmpf,))
                b.op("v", lambda e, c=c, ho=ho, hl=hl: e.tensor_tensor(
                    ln2[:, c * HALF + ho:c * HALF + ho + hl], tmpf[:, 0:hl],
                    B_t[:, t0 + ho:t0 + ho + hl], ALU.add),
                    reads=(r_tmpf, r_B), war=(r_ln2,), sets=(r_ln2,))
        glog, sm, swk, msk, mx1, rs1 = g_sc
        g0, g1, g2, g3, g4, g5 = r_gsc
        for tt in range(HALF // 128):
            for c in range(NC_D):
                b.op("t", lambda e, tt=tt, c=c: e.matmul(
                    psD[:, 0:16], ln2[:, c * HALF + tt * 128:c * HALF + (tt + 1) * 128],
                    gw_b[:, c * 16:(c + 1) * 16],
                    start=(c == 0), stop=(c == NC_D - 1)),
                    reads=(r_ln2, d_gwc),
                    war=((r_psD,) if c == 0 else ()),
                    sets=((r_psD,) if c == NC_D - 1 else ()), tick=(c == NC_D - 1))
            b.op("v", lambda e: e.reduce_max(mx1[:, 0:1], psD[:, 0:16], axis=AX.X,
                                             negate=True),
                 reads=(r_psD,), war=(g4,), sets=(g4,))
            b.op("s", lambda e: e.activation(glog[:, :], psD[:, 0:16], AF.Exp,
                                             bias=mx1[:, 0:1]),
                 reads=(r_psD, g4), war=(g0,), sets=(g0,))
            b.op("v", lambda e: e.reduce_sum(rs1[:, 0:1], glog[:, :], axis=AX.X),
                 reads=(g0,), war=(g5,), sets=(g5,))
            b.op("v", lambda e: e.reciprocal(rs1[:, 0:1], rs1[:, 0:1]),
                 reads=(g5,), sets=(g5,))
            b.op("v", lambda e: e.tensor_scalar(sm[:, :], glog[:, :], rs1[:, 0:1], None,
                                                ALU.mult),
                 reads=(g0, g5), war=(g1,), sets=(g1,))
            b.op("v", lambda e: e.tensor_copy(swk[:, :], sm[:, :]),
                 reads=(g1,), war=(g2,), sets=(g2,))
            for it in range(6):
                b.op("v", lambda e: e.reduce_max(mx1[:, 0:1], swk[:, :], axis=AX.X),
                     reads=(g2,), war=(g4,), sets=(g4,))
                if it < 5:
                    b.op("v", lambda e: e.tensor_scalar(mski[:, :], swk[:, :], mx1[:, 0:1],
                                                        None, ALU.is_ge),
                         reads=(g2, g4), war=(g3,), sets=(g3,))
                    b.op("v", lambda e: e.select(swk[:, :], mski[:, :], negt[:, :],
                                                 swk[:, :], add_drain=True),
                         reads=(g3, r_cst, g2), sets=(g2,))
            b.op("v", lambda e: e.tensor_scalar(msk[:, :], sm[:, :], mx1[:, 0:1], None,
                                                ALU.is_ge),
                 reads=(g1, g4), war=(g3,), sets=(g3,))
            b.op("v", lambda e: e.tensor_tensor(swk[:, :], sm[:, :], msk[:, :], ALU.mult),
                 reads=(g1, g3), sets=(g2,))
            b.op("t", lambda e: e.transpose(psE[0:16, 0:128], swk[:, :], idn[:, :]),
                 reads=(g2, d_cst), war=(r_psE,), sets=(r_psE,))
            b.op("v", lambda e, tt=tt: e.tensor_copy(
                cwT[0:16, t0 + tt * 128:t0 + (tt + 1) * 128], psE[0:16, 0:128]),
                reads=(r_psE,), war=(r_cwT,), sets=(r_cwT,))

    def ffn_half(t0):
        TCH = _chunks(0, HALF, 512)
        for ex in range(E):
            for ci, (co, cl) in enumerate(TCH):
                ps, rps = (psD, r_psD) if ci % 2 == 0 else (psE, r_psE)
                b.op("t", lambda e, ps=ps, co=co, cl=cl, ex=ex: e.matmul(
                    ps[:, 0:cl], selm[0:16, ex * 128:(ex + 1) * 128],
                    cwT[0:16, t0 + co:t0 + co + cl],
                    start=True, stop=True),
                    reads=(r_cwT, d_cst), war=(rps,), sets=(rps,))
                b.op("s", lambda e, ps=ps, co=co, cl=cl: e.activation(
                    cwb[:, co:co + cl], ps[:, 0:cl], AF.Identity),
                    reads=(rps,), war=(r_cwb,), sets=(r_cwb,))
            for m in range(NC_M):
                img = (ex * NC_M + m) % 2
                b.dma("g", wgb[img][:, :].rearrange("p (k f) -> p k f", k=NC_D),
                      wg[ex, m, :, :, :].rearrange("k p f -> p k f"), d_wgb[img])
                b.dma("g", wub[img][:, :].rearrange("p (k f) -> p k f", k=NC_D),
                      wu[ex, m, :, :, :].rearrange("k p f -> p k f"), d_wub[img])
                for k in range(NC_D):
                    for ci, (co, cl) in enumerate(TCH):
                        b.op("t", lambda e, k=k, co=co, cl=cl, ci=ci, img=img: e.matmul(
                            psA[:, ci * 512:ci * 512 + cl],
                            wgb[img][:, k * 128:(k + 1) * 128],
                            ln2[:, k * HALF + co:k * HALF + co + cl],
                            start=(k == 0), stop=(k == NC_D - 1)),
                            reads=(d_wgb[img], r_ln2),
                            war=((r_psA[ci],) if k == 0 else ()),
                            sets=((r_psA[ci],) if k == NC_D - 1 else ()),
                            tick=(k == NC_D - 1))
                for k in range(NC_D):
                    for ci, (co, cl) in enumerate(TCH):
                        b.op("t", lambda e, k=k, co=co, cl=cl, ci=ci, img=img: e.matmul(
                            psB[:, ci * 512:ci * 512 + cl],
                            wub[img][:, k * 128:(k + 1) * 128],
                            ln2[:, k * HALF + co:k * HALF + co + cl],
                            start=(k == 0), stop=(k == NC_D - 1)),
                            reads=(d_wub[img], r_ln2),
                            war=((r_psB[ci],) if k == 0 else ()),
                            sets=((r_psB[ci],) if k == NC_D - 1 else ()),
                            tick=(k == NC_D - 1))
                rA = (r_psA[0], r_psA[1]) if len(TCH) > 1 else (r_psA[0],)
                rB = (r_psB[0], r_psB[1]) if len(TCH) > 1 else (r_psB[0],)
                b.op("s", lambda e: e.activation(gs_t[:, 0:HALF], psA[:, 0:HALF],
                                                 AF.Silu),
                     reads=rA, war=(r_gs,), sets=(r_gs,))
                b.op("v", lambda e: e.scalar_tensor_tensor(
                    us_t[:, 0:HALF], psB[:, 0:HALF], 1.0, cwb[:, 0:HALF],
                    ALU.mult, ALU.mult),
                    reads=rB + (r_cwb,), war=(r_us,), sets=(r_us,))
                b.op("v", lambda e, m=m: e.tensor_tensor(
                    t_e[:, m * HALF:(m + 1) * HALF], gs_t[:, 0:HALF], us_t[:, 0:HALF],
                    ALU.mult),
                    reads=(r_gs, r_us), war=(r_te,), sets=(r_te,))
            for j in range(NC_D):
                ij = (ex * NC_D + j) % 2
                b.dma("g", wdb[ij][:, :].rearrange("p (m f) -> p m f", m=NC_M),
                      wd[ex, j, :, :, :].rearrange("m p f -> p m f"), d_wdb[ij])
                for m in range(NC_M):
                    for ci, (co, cl) in enumerate(TCH):
                        b.op("t", lambda e, m=m, co=co, cl=cl, ci=ci, ij=ij: e.matmul(
                            psC[:, ci * 512:ci * 512 + cl],
                            wdb[ij][:, m * 128:(m + 1) * 128],
                            t_e[:, m * HALF + co:m * HALF + co + cl],
                            start=(m == 0), stop=(m == NC_M - 1)),
                            reads=(d_wdb[ij], r_te),
                            war=((r_psC[ci],) if m == 0 else ()),
                            sets=((r_psC[ci],) if m == NC_M - 1 else ()),
                            tick=(m == NC_M - 1))
                for ci, (co, cl) in enumerate(TCH):
                    b.op("v", lambda e, j=j, co=co, cl=cl, ci=ci: e.tensor_tensor(
                        x1[:, j * HALF + co:j * HALF + co + cl],
                        x1[:, j * HALF + co:j * HALF + co + cl],
                        psC[:, ci * 512:ci * 512 + cl], ALU.add),
                        reads=(r_psC[ci], r_x1), sets=(r_x1,))
        for c in range(NC_D):
            b.stream_write(out_t[c * 128:(c + 1) * 128, t0:t0 + HALF],
                           x1[:, c * HALF:(c + 1) * HALF], r_x1)

    b.fence(stream_resses=(r_x1st[0], r_x1st[1]))
    ln2_gate(0)
    ffn_half(0)
    b.barrier_y(r_x1st)
    b.barrier_y([r_x1])
    for c in range(NC_D):
        i = c % 2
        b.dma("y", x1[:, c * HALF:(c + 1) * HALF], x1b_d[c * 128:(c + 1) * 128, :],
              d_x1r[i], reads=(r_x1,))
    r_x1.producer = ("y", d_x1r[1].dmasem, d_x1r[1].dmacnt)
    r_x1.readers = [("y", d_x1r[0].dmasem, d_x1r[0].dmacnt)]
    r_x1_extra = Res("x1extra")
    r_x1_extra.producer = ("y", d_x1r[0].dmasem, d_x1r[0].dmacnt)
    b.op("v", lambda e: e.engine_nop(), reads=(r_x1_extra,))
    b.fence(stream_resses=(r_x1,))
    ln2_gate(HALF)
    ffn_half(HALF)
    b.barrier_y([r_x1])

    with nc.Block() as block:
        @block.tensor
        def _(e):
            for f in b.q["t"]:
                f(e)

        @block.scalar
        def _(e):
            for f in b.q["s"]:
                f(e)

        @block.vector
        def _(e):
            for f in b.q["v"]:
                f(e)

        @block.gpsimd
        def _(e):
            for f in b.q["g"]:
                f(e)

        @block.sync
        def _(e):
            for f in b.q["y"]:
                f(e)

    b.close_sems()
    return nc



def _eo_perm():
    p = []
    for hh in range(NH):
        base = hh * HDIM
        p += list(range(base, base + HDIM, 2)) + list(range(base + 1, base + HDIM, 2))
    return np.array(p)


def _tile_km(w, nk, nm):
    return np.ascontiguousarray(w.reshape(nk, 128, nm, 128).transpose(0, 2, 1, 3))


def host_prep(ins, S):
    perm = _eo_perm()
    d = {}
    d["wdkv"] = _tile_km(ins["L0_attn_W_DKV"], NC_D, NC_KV)
    d["wuk"] = _tile_km(ins["L0_attn_W_UK"][:, perm], NC_KV, NH)
    d["wuv"] = _tile_km(ins["L0_attn_W_UV"], NC_KV, NH)
    d["wdq"] = _tile_km(ins["L0_attn_W_DQ"], NC_D, NC_Q)
    d["wuq"] = _tile_km(ins["L0_attn_W_UQ"][:, perm], NC_Q, NH)
    d["wkr"] = _tile_km(ins["L0_attn_W_KR"][:, perm], NC_D, NC_D)
    wot = _tile_km(ins["L0_attn_W_O"], NC_D, NC_D)
    d["wo"] = np.ascontiguousarray(
        wot.reshape(NC_D, NC_D // 4, 4, 128, 128).transpose(1, 0, 2, 3, 4))
    d["gw"] = np.ascontiguousarray(ins["L0_ffn_gate_w"].T.reshape(NC_D, 128, 16))
    wgp = ins["L0_ffn_w_gate"].transpose(0, 2, 1)
    wup = ins["L0_ffn_w_up"].transpose(0, 2, 1)
    wdp = ins["L0_ffn_w_down"].transpose(0, 2, 1)
    d["wg"] = np.ascontiguousarray(
        wgp.reshape(E, NC_D, 128, NC_M, 128).transpose(0, 3, 1, 2, 4))
    d["wu"] = np.ascontiguousarray(
        wup.reshape(E, NC_D, 128, NC_M, 128).transpose(0, 3, 1, 2, 4))
    d["wd"] = np.ascontiguousarray(
        wdp.reshape(E, NC_M, 128, NC_D, 128).transpose(0, 3, 1, 2, 4))
    bpk = np.zeros((128, 98), dtype=np.float32)
    bpk[:, 97] = 1.0

    def put(col, vec):
        n = len(vec) // 128
        bpk[:, col:col + n] = vec.reshape(n, 128).T
    put(0, ins["L0_attn_b_DKV"])
    put(4, ins["L0_attn_b_UK"][perm])
    put(20, ins["L0_attn_b_UV"])
    put(36, ins["L0_attn_b_DQ"])
    put(48, ins["L0_attn_b_UQ"][perm])
    put(64, ins["L0_attn_b_KR"][perm])
    put(80, ins["L0_attn_b_O"])
    d["biasp"] = bpk
    pos = np.arange(S, dtype=np.float64)
    invf = 1.0 / (10000.0 ** (np.arange(64, dtype=np.float64) * 2.0 / 128))
    ang = invf[:, None] * pos[None, :]
    cosd, sind = np.cos(ang), np.sin(ang)
    d["t1d"] = np.concatenate([cosd, cosd], axis=0).astype(np.float32)
    d["t2d"] = np.concatenate([-sind, sind], axis=0).astype(np.float32)
    d["cst_ao_f"] = np.ones((128, 128), dtype=np.float32)
    d["cst_ao_b"] = np.ones((128, 128), dtype=BF)
    d["cst_id"] = np.eye(128, dtype=np.float32)
    pim = np.zeros((128, 128), dtype=np.float32)
    for m in range(128):
        pim[(m + 64) % 128, m] = 1.0
    d["cst_pi"] = pim.astype(BF)
    tri = (np.arange(128)[:, None] <= np.arange(128)[None, :]).astype(np.float32)
    d["cst_tri"] = tri.astype(BF)
    d["cst_o1b"] = np.ones((1, 128), dtype=BF)
    sel = np.zeros((16, 16 * 128), dtype=np.float32)
    for ex in range(16):
        sel[ex, ex * 128:(ex + 1) * 128] = 1.0
    d["cst_sel"] = sel.astype(BF)
    return d


def kernel(**inputs):
    S = 2048
    ncore = 8
    shared = host_prep(inputs, S)
    h = inputs["L0_attn_h"]
    nc = build_program(S)
    in_maps = []
    for c in range(ncore):
        m = dict(shared)
        m["h_t"] = np.ascontiguousarray(h[c].T)
        in_maps.append(m)
    res = run_bass_kernel_spmd(nc, in_maps, core_ids=list(range(ncore)))
    out = np.empty((ncore, S, D), dtype=np.float32)
    for c in range(ncore):
        out[c] = res.results[c]["out_t"].T
    return out


# version 13
# speedup vs baseline: 1.0231x; 1.0231x over previous
import numpy as np
import ml_dtypes
import concourse.bass as bass
import concourse.mybir as mybir
from concourse.bass_utils import run_bass_kernel_spmd

F32 = mybir.dt.float32
BF16 = mybir.dt.bfloat16
AF = mybir.ActivationFunctionType
ALU = mybir.AluOpType
AX = mybir.AxisListType
BF = ml_dtypes.bfloat16

D = 2048
NH = 16
HDIM = 128
DKV = 512
DQ = 1536
E = 16
MD = 1408
NC_D = D // 128
NC_KV = DKV // 128
NC_Q = DQ // 128
NC_M = MD // 128
EPS = 1e-5
SM_SCALE = 1.0 / 16.0

ENGS = ("t", "s", "v", "g", "y")


def _chunks(lo, hi, w):
    out = []
    while lo < hi:
        out.append((lo, min(w, hi - lo)))
        lo += w
    return out


class Res:

    def __init__(self, name):
        self.name = name
        self.producer = None
        self.readers = []


class Build:
    def __init__(self, nc):
        self.nc = nc
        self.q = {k: [] for k in ENGS}
        self.sems = {}
        self.pn = {k: 0 for k in ("t", "s", "v")}
        self.psem = {}
        self.waited = {k: {} for k in ENGS}
        self.clean = {k: 0 for k in ENGS}
        self.epoch = {k: 0 for k in ENGS}
        self._sem_ctxs = []
        self.pend = {}
        self.default_reads = ()
        for e in ("t", "s", "v"):
            self.psem[e] = self.sem("p_" + e)

    def sem(self, name):
        cm = self.nc.semaphore(name)
        h = cm.__enter__()
        self._sem_ctxs.append(cm)
        self.sems[name] = h
        return h

    def close_sems(self):
        for cm in self._sem_ctxs:
            cm.__exit__(None, None, None)

    def _dep_wait(self, eng, raw_deps, war_deps):
        need_drain = False
        waits = []
        for (se, sem, val) in raw_deps:
            if se == eng:
                need_drain = True
            else:
                key = id(sem)
                if self.waited[eng].get(key, -1) < val:
                    self.waited[eng][key] = val
                    waits.append((sem, val))
        for (se, sem, val) in war_deps:
            if se == eng:
                need_drain = True
                continue
            key = id(sem)
            if self.waited[eng].get(key, -1) < val:
                self.waited[eng][key] = val
                waits.append((sem, val))
        if need_drain and self.clean[eng] < self.epoch[eng]:
            self.q[eng].append(lambda e: e.drain())
            self.clean[eng] = self.epoch[eng]
        for (sem, val) in waits:
            self.q[eng].append(lambda e, s=sem, v=val: e.wait_ge(s, v))

    def op(self, eng, fn, reads=(), war=(), sets=(), tick=True):
        raw_deps = []
        war_deps = []
        reads = tuple(reads) + getattr(self, "default_reads", ())
        for r in reads:
            if r.producer:
                raw_deps.append(r.producer)
        for w in war:
            war_deps.extend(w.readers)
            if w.producer and not w.readers:
                war_deps.append(w.producer)
        self._dep_wait(eng, raw_deps, war_deps)
        self.epoch[eng] += 1
        if not tick:
            assert not sets
            self.q[eng].append(lambda e, f=fn: f(e))
            self.pend.setdefault(eng, []).extend(reads)
            return
        self.pn[eng] += 1
        val = self.pn[eng]
        sem = self.psem[eng]
        self.q[eng].append(lambda e, f=fn, s=sem: f(e).then_inc(s, 1))
        stamp = (eng, sem, val)
        for w in war:
            w.readers = []
        for s_ in sets:
            s_.producer = stamp
            s_.readers = []
        for r in reads:
            r.readers.append(stamp)
        for r in self.pend.get(eng, ()):
            r.readers.append(stamp)
        self.pend[eng] = []

    def mkbuf(self, name):
        r = Res(name)
        r.dmasem = self.sem("d_" + name)
        r.dmacnt = 0
        return r

    def dma(self, eng, out_ap, in_ap, buf, reads=()):
        raw_deps = []
        for r in reads:
            if r.producer:
                raw_deps.append(r.producer)
        war_deps = list(buf.readers)
        if buf.producer and not buf.readers:
            raw_deps.append(buf.producer)
        self._dep_wait(eng, raw_deps, war_deps)
        self.epoch[eng] += 1
        buf.dmacnt += 16
        cnt = buf.dmacnt
        self.q[eng].append(lambda e, o=out_ap, i=in_ap, s=buf.dmasem:
                           e.dma_start(out=o, in_=i).then_inc(s, 16))
        buf.producer = (eng, buf.dmasem, cnt)
        buf.readers = []

    def stream_write(self, out_ap, in_ap, src_res):
        if not hasattr(src_res, "wsem"):
            src_res.wsem = self.sem("w_" + src_res.name)
            src_res.wcnt = 0
        raw_deps = [src_res.producer] if src_res.producer else []
        self._dep_wait("y", raw_deps, [])
        self.epoch["y"] += 1
        src_res.wcnt += 16
        self.q["y"].append(lambda e, o=out_ap, i=in_ap, s=src_res.wsem:
                           e.dma_start(out=o, in_=i).then_inc(s, 16))
        src_res.readers.append(("y", src_res.wsem, src_res.wcnt))

    def fence(self, stream_resses=()):
        vals = {e: self.pn[e] for e in ("t", "s", "v")}
        wdeps = []
        for r in stream_resses:
            if hasattr(r, "wsem"):
                wdeps.append(("y", r.wsem, r.wcnt))
        for eng in ENGS:
            deps = [(oe, self.psem[oe], vals[oe])
                    for oe in ("t", "s", "v") if oe != eng and vals[oe] > 0]
            deps += [d for d in wdeps if d[0] != eng or eng != "y"]
            if eng == "y":
                deps = [d for d in deps if d[1] is not None]
            self._dep_wait(eng, deps, [])
            if eng in ("t", "s", "v"):
                self.q[eng].append(lambda e: e.drain())
                self.clean[eng] = self.epoch[eng]
                self.epoch[eng] += 1

    def barrier_y(self, res_list):
        for r in res_list:
            if hasattr(r, "wsem"):
                self.q["y"].append(
                    lambda e, s=r.wsem, v=r.wcnt: e.wait_ge(s, v))


def build_program(S):
    HALF = S // 2
    QCS = _chunks(0, S, min(512, S // 2))
    nc = bass.Bass("TRN2", target_bir_lowering=False)

    dt = nc.dram_tensor
    h_t = dt("h_t", [D, S], F32, kind="ExternalInput").ap()
    wdkv = dt("wdkv", [NC_D, NC_KV, 128, 128], F32, kind="ExternalInput").ap()
    wuk = dt("wuk", [NC_KV, NH, 128, 128], F32, kind="ExternalInput").ap()
    wuv = dt("wuv", [NC_KV, NH, 128, 128], F32, kind="ExternalInput").ap()
    wdq = dt("wdq", [NC_D, NC_Q, 128, 128], F32, kind="ExternalInput").ap()
    wuq = dt("wuq", [NC_Q, NH, 128, 128], F32, kind="ExternalInput").ap()
    wkr = dt("wkr", [NC_D, NC_D, 128, 128], F32, kind="ExternalInput").ap()
    wo = dt("wo", [NC_D // 4, NC_D, 4, 128, 128], F32, kind="ExternalInput").ap()
    gw = dt("gw", [NC_D, 128, 16], F32, kind="ExternalInput").ap()
    wg = dt("wg", [E, NC_M, NC_D, 128, 128], F32, kind="ExternalInput").ap()
    wu = dt("wu", [E, NC_M, NC_D, 128, 128], F32, kind="ExternalInput").ap()
    wd = dt("wd", [E, NC_D, NC_M, 128, 128], F32, kind="ExternalInput").ap()
    biasp = dt("biasp", [128, 98], F32, kind="ExternalInput").ap()
    t1d = dt("t1d", [128, S], F32, kind="ExternalInput").ap()
    t2d = dt("t2d", [128, S], F32, kind="ExternalInput").ap()
    cst_ao_f = dt("cst_ao_f", [128, 128], F32, kind="ExternalInput").ap()
    cst_ao_b = dt("cst_ao_b", [128, 128], BF16, kind="ExternalInput").ap()
    cst_id = dt("cst_id", [128, 128], F32, kind="ExternalInput").ap()
    cst_pi = dt("cst_pi", [128, 128], BF16, kind="ExternalInput").ap()
    cst_tri = dt("cst_tri", [128, 128], BF16, kind="ExternalInput").ap()
    cst_o1b = dt("cst_o1b", [1, 128], BF16, kind="ExternalInput").ap()
    cst_sel = dt("cst_sel", [16, 16 * 128], BF16, kind="ExternalInput").ap()
    out_t = dt("out_t", [D, S], F32, kind="ExternalOutput").ap()

    ln1_d = dt("ln1_d", [D, S], BF16, kind="Internal").ap()
    cq_d = dt("cq_d", [DQ, S], BF16, kind="Internal").ap()
    ctx_d = dt("ctx_d", [D, S], BF16, kind="Internal").ap()
    x1b_d = dt("x1b_d", [D, HALF], F32, kind="Internal").ap()

    BC_DKV, BC_UK, BC_UV, BC_DQ, BC_UQ, BC_KR, BC_O = 0, 4, 20, 36, 48, 64, 80

    SB_BASE = 16384 + 1024
    off = [SB_BASE]

    def sb(name, shape, dtv, at=None):
        nbytes = int(np.prod(shape[1:])) * (4 if dtv == F32 else 2)
        if at is None:
            at = off[0]
            off[0] = at + ((nbytes + 63) // 64) * 64
        return nc.alloc_sbuf_tensor_at(name, shape, dtv, offset=at).ap()

    A_t = sb("A_t", [128, S], F32)
    B_t = sb("B_t", [128, S], F32)
    cwT = sb("cwT", [16, S], BF16)
    ao_f = sb("ao_f", [128, 128], F32)
    ao_b = sb("ao_b", [128, 128], BF16)
    idn = sb("idn", [128, 128], F32)
    pi_b = sb("pi_b", [128, 128], BF16)
    tri_b = sb("tri_b", [128, 128], BF16)
    o1b = sb("o1b", [1, 128], BF16)
    selm = sb("selm", [16, 16 * 128], BF16)
    bp = sb("bp", [128, 98], F32)
    nc.const_aps.register(F32, 0.0, bp[:, 96:97])
    nc.const_aps.register(F32, 1.0, bp[:, 97:98])
    gw_b = sb("gw_b", [128, 16 * 16], BF16)
    g_sc = [sb(f"g_sc{i}", [128, 16], F32) for i in range(6)]
    mski = sb("mski", [128, 16], mybir.dt.int32)
    negt = sb("negt", [128, 16], F32)

    REG1 = off[0]
    kR = sb("kR", [128, NC_D * S], BF16, at=REG1)
    x1 = sb("x1", [128, NC_D * HALF], F32, at=REG1)
    T1 = sb("T1", [128, S], F32, at=REG1 + NC_D * S * 2)
    T2 = sb("T2", [128, S], F32, at=REG1 + NC_D * S * 2 + S * 4)
    p4o = REG1 + NC_D * S * 2
    hbuf = [sb(f"hbuf{i}", [128, 512], F32, at=p4o + i * 2048) for i in range(2)]
    cxbuf = [sb(f"cxbuf{i}", [128, 512], BF16,
                at=p4o + (4096 + i * 1024 if i < 2 else 12288 + (i - 2) * 1024))
             for i in range(4)]
    x1st = [sb(f"x1st{i}", [128, 512], F32, at=p4o + 6144 + i * 2048) for i in range(2)]
    xsq = sb("xsq", [128, 512], F32, at=p4o + 10240)
    off[0] = REG1 + max(NC_D * S * 2 + S * 8, NC_D * HALF * 4 + 14336)

    REG2 = off[0]
    ln2 = sb("ln2", [128, NC_D * HALF], BF16, at=REG2)
    gs_t = sb("gs_t", [128, HALF], BF16, at=REG2 + NC_D * HALF * 2)
    us_t = sb("us_t", [128, HALF], BF16, at=REG2 + NC_D * HALF * 2 + HALF * 2)
    cwb = sb("cwb", [128, HALF], BF16, at=REG2 + NC_D * HALF * 2 + HALF * 4)
    cqst = [sb(f"cqst{i}", [128, 512], BF16, at=REG2 + i * 1024) for i in range(2)]
    krt = [sb(f"krt{i}", [128, 512], BF16, at=REG2 + 2048 + i * 1024) for i in range(2)]
    rp1 = sb("rp1", [128, 512], BF16, at=REG2 + 4096)
    rp2 = sb("rp2", [128, 512], BF16, at=REG2 + 5120)
    cqr = [sb(f"cqr{i}", [128, NC_Q * 512], BF16, at=REG2 + 6144 + i * NC_Q * 1024)
           for i in range(2)]
    off[0] = REG2 + max(NC_D * HALF * 2 + HALF * 6, 6144 + 2 * NC_Q * 1024)

    REG3 = off[0]
    cKV = sb("cKV", [128, NC_KV * S], BF16, at=REG3)
    t_e = sb("t_e", [128, NC_M * HALF], BF16, at=REG3)
    st_sc = [sb(f"st_sc{i}", [128, 512], F32, at=REG3 + i * 2048) for i in range(4)]
    off[0] = REG3 + max(NC_KV * S * 2 + 2048, NC_M * HALF * 2, 4 * 2048)

    REG4 = off[0]
    xb = [sb(f"xb{i}", [128, 1024], F32, at=REG4 + i * 4096) for i in range(4)]
    lnb = [sb(f"lnb{i}", [128, 1024], BF16, at=REG4 + 8192 + i * 2048) for i in range(2)]
    tmpf = sb("tmpf", [128, 1024], F32, at=REG4 + 12288)
    qC_h = sb("qC_h", [128, S], BF16, at=REG4)
    qR_h = sb("qR_h", [128, S], BF16, at=REG4 + S * 2)
    kC_h = sb("kC_h", [128, S], BF16, at=REG4 + S * 4)
    v_h = sb("v_h", [128, S], BF16, at=REG4 + S * 6)
    expb = [sb(f"expb{i}", [128, 512], BF16, at=REG4 + S * 8 + i * 1024) for i in range(4)]
    cxst = [sb(f"cxst{i}", [128, 512], BF16, at=REG4 + S * 8 + 4096 + i * 1024)
            for i in range(2)]
    rcp = sb("rcp", [128, 512], F32, at=REG3 + NC_KV * S * 2)
    wdb = [sb(f"wdb{i}", [128, NC_M * 128], BF16, at=REG4 + i * NC_M * 256) for i in range(2)]
    off[0] = REG4 + max(16384, S * 8 + 6144, 2 * NC_M * 256)

    REG5 = off[0]
    wb_p2 = [sb(f"wb_p2{i}", [128, 6 * 128], BF16, at=REG5 + i * 1536) for i in range(4)]
    lrhs = [sb(f"lrhs{i}", [128, 512], BF16, at=REG5 + 6144 + i * 1024) for i in range(4)]
    wuq_b = [sb(f"wuq_b{i}", [128, NC_Q * 128], BF16, at=REG5 + i * NC_Q * 256)
             for i in range(2)]
    wukv_b = [sb(f"wukv_b{i}", [128, 2 * NC_KV * 128], BF16,
                 at=REG5 + 2 * NC_Q * 256 + i * 2 * NC_KV * 256) for i in range(2)]
    wo_b = [sb(f"wo_b{i}", [128, 8 * 4 * 128], BF16, at=REG5 + i * 8 * 4 * 256)
            for i in range(2)]
    wgb = [sb(f"wgb{i}", [128, NC_D * 128], BF16, at=REG5 + i * NC_D * 256) for i in range(2)]
    wub = [sb(f"wub{i}", [128, NC_D * 128], BF16, at=REG5 + 2 * NC_D * 256 + i * NC_D * 256)
           for i in range(2)]
    off[0] = REG5 + max(5120, 2 * NC_Q * 256 + 4 * NC_KV * 256, 2 * 8 * 4 * 256,
                        4 * NC_D * 256)
    assert off[0] <= 229376, f"SBUF over budget: {off[0]} B/partition"

    psA = nc.alloc_psum_tensor("psA", [128, 1024], F32).ap()
    psB = nc.alloc_psum_tensor("psB", [128, 1024], F32).ap()
    psC = nc.alloc_psum_tensor("psC", [128, 1024], F32).ap()
    psD = nc.alloc_psum_tensor("psD", [128, 512], F32).ap()
    psE = nc.alloc_psum_tensor("psE", [128, 512], F32).ap()

    b = Build(nc)
    R = lambda n: Res(n)
    r_psA = [R("psA0"), R("psA1")]
    r_psB = [R("psB0"), R("psB1")]
    r_psC = [R("psC0"), R("psC1")]
    r_psD, r_psE = R("psD"), R("psE")
    r_A, r_B = R("A"), R("B")
    r_cKV, r_kR = R("cKV"), R("kR")
    r_kC, r_v = R("kC"), R("v")
    r_x1, r_ln2, r_te, r_cwT, r_cwb = R("x1"), R("ln2"), R("te"), R("cwT"), R("cwb")
    r_gs, r_us = R("gs"), R("us")
    r_expb = [R(f"expb{i}") for i in range(4)]
    r_rcp = R("rcp")
    r_cxst = [R("cxst0"), R("cxst1")]
    r_krt = [R("krt0"), R("krt1")]
    r_rp1, r_rp2 = R("rp1"), R("rp2")
    r_cqst = [R("cqst0"), R("cqst1")]
    r_x1st = [R("x1st0"), R("x1st1")]
    r_xsq = R("xsq")
    r_stsc = [R(f"stsc{i}") for i in range(4)]
    r_gsc = [R(f"gsc{i}") for i in range(6)]
    r_tmpf = R("tmpf")
    r_lnb = [R("lnb0"), R("lnb1")]
    r_cst = R("consts")

    d_xb = [b.mkbuf(f"xb{i}") for i in range(4)]
    d_lrhs = [b.mkbuf(f"lrhs{i}") for i in range(4)]
    d_wbp2 = [b.mkbuf(f"wbp2{i}") for i in range(4)]
    d_cqr = [b.mkbuf("cqr0"), b.mkbuf("cqr1")]
    d_wuq = [b.mkbuf("wuq0"), b.mkbuf("wuq1")]
    d_wukv = [b.mkbuf("wukv0"), b.mkbuf("wukv1")]
    d_wo = [b.mkbuf("wo0"), b.mkbuf("wo1")]
    d_hb = [b.mkbuf("hb0"), b.mkbuf("hb1")]
    d_cx = [b.mkbuf(f"cx{i}") for i in range(4)]
    d_wgb = [b.mkbuf("wgb0"), b.mkbuf("wgb1")]
    d_wub = [b.mkbuf("wub0"), b.mkbuf("wub1")]
    d_wdb = [b.mkbuf("wdb0"), b.mkbuf("wdb1")]
    d_cst = b.mkbuf("cst")
    d_gwc = b.mkbuf("gwc")
    d_x1r = [b.mkbuf("x1r0"), b.mkbuf("x1r1")]


    for dst, src in [(bp, biasp), (ao_f, cst_ao_f), (ao_b, cst_ao_b), (idn, cst_id),
                     (pi_b, cst_pi), (tri_b, cst_tri), (o1b, cst_o1b),
                     (selm, cst_sel), (T1, t1d), (T2, t2d)]:
        b.dma("y", dst[:, :], src[:, :], d_cst)
    for c in range(NC_D):
        b.dma("g", gw_b[:, c * 16:(c + 1) * 16], gw[c, :, :], d_gwc)
    b.op("v", lambda e: e.memset(negt[:, :], -1e30), sets=(r_cst,))
    b.default_reads = (d_cst,)

    def stats_chain(sl, slen, ps_sum_ap, r_sum, ps_sq_ap, r_sq):
        mt, e2, ve, sc3 = st_sc
        m0, m1, m2, m3 = r_stsc
        SL = slice(sl, sl + slen)
        L = slice(0, slen)
        b.op("v", lambda e: e.tensor_scalar(mt[:, L], ps_sum_ap[:, L], 1.0 / D, None,
                                            ALU.mult),
             reads=(r_sum,), war=(m0,), sets=(m0,))
        b.op("v", lambda e: e.tensor_scalar(e2[:, L], ps_sq_ap[:, L], 1.0 / D, None,
                                            ALU.mult),
             reads=(r_sq,), war=(m1,), sets=(m1,))
        b.op("v", lambda e: e.scalar_tensor_tensor(ve[:, L], mt[:, L], -1.0, mt[:, L],
                                                   ALU.mult, ALU.mult),
             reads=(m0,), war=(m2,), sets=(m2,))
        b.op("v", lambda e: e.tensor_tensor(ve[:, L], ve[:, L], e2[:, L], ALU.add),
             reads=(m2, m1), sets=(m2,))
        b.op("v", lambda e: e.tensor_scalar(ve[:, L], ve[:, L], EPS, None, ALU.add),
             reads=(m2,), sets=(m2,))
        b.op("s", lambda e: e.activation(sc3[:, L], ve[:, L], AF.Sqrt),
             reads=(m2,), war=(m3,), sets=(m3,))
        b.op("v", lambda e: e.reciprocal(sc3[:, L], sc3[:, L]),
             reads=(m3,), sets=(m3,))
        b.op("v", lambda e: e.tensor_tensor(A_t[:, SL], sc3[:, L], sc3[:, L], ALU.mult),
             reads=(m3,), war=(r_A,), sets=(r_A,))
        b.op("v", lambda e: e.tensor_tensor(A_t[:, SL], A_t[:, SL], ve[:, L], ALU.mult),
             reads=(r_A, m2), sets=(r_A,))
        b.op("v", lambda e: e.tensor_scalar(A_t[:, SL], A_t[:, SL], -0.5, 1.5,
                                            ALU.mult, ALU.add),
             reads=(r_A,), sets=(r_A,))
        b.op("v", lambda e: e.tensor_tensor(A_t[:, SL], A_t[:, SL], sc3[:, L], ALU.mult),
             reads=(r_A, m3), sets=(r_A,))
        b.op("v", lambda e: e.scalar_tensor_tensor(B_t[:, SL], mt[:, L], -1.0, A_t[:, SL],
                                                   ALU.mult, ALU.mult),
             reads=(m0, r_A), war=(r_B,), sets=(r_B,))

    for (qo, ql) in _chunks(0, S, 512):
        for c in range(NC_D):
            i = c % 4
            ix = 2 + (c % 2)
            b.dma("y", xb[i][:, 0:ql], h_t[c * 128:(c + 1) * 128, qo:qo + ql], d_xb[i])
            b.op("s", lambda e, i=i, ql=ql, ix=ix: e.activation(
                st_sc[ix][:, 0:ql], xb[i][:, 0:ql], AF.Square),
                reads=(d_xb[i],), war=(r_stsc[ix],), sets=(r_stsc[ix],))
            b.op("t", lambda e, i=i, ql=ql, c=c: e.matmul(
                psD[:, 0:ql], ao_f[:, :], xb[i][:, 0:ql],
                start=(c == 0), stop=(c == NC_D - 1)),
                reads=(d_xb[i], d_cst),
                war=((r_psD,) if c == 0 else ()),
                sets=((r_psD,) if c == NC_D - 1 else ()), tick=(c == NC_D - 1))
            b.op("t", lambda e, ql=ql, c=c, ix=ix: e.matmul(
                psE[:, 0:ql], ao_f[:, :], st_sc[ix][:, 0:ql],
                start=(c == 0), stop=(c == NC_D - 1)),
                reads=(r_stsc[ix],),
                war=((r_psE,) if c == 0 else ()),
                sets=((r_psE,) if c == NC_D - 1 else ()), tick=True)
        stats_chain(qo, ql, psD, r_psD, psE, r_psE)

    def p1_block(qo, ql):
        for c in range(NC_D):
            i = c % 2
            b.dma("y", xb[i][:, 0:ql], h_t[c * 128:(c + 1) * 128, qo:qo + ql], d_xb[i])
            b.op("v", lambda e, i=i, qo=qo, ql=ql: e.tensor_tensor(
                tmpf[:, 0:ql], xb[i][:, 0:ql], A_t[:, qo:qo + ql], ALU.mult),
                reads=(d_xb[i], r_A), war=(r_tmpf,), sets=(r_tmpf,))
            b.op("v", lambda e, i=i, qo=qo, ql=ql: e.tensor_tensor(
                lnb[i][:, 0:ql], tmpf[:, 0:ql], B_t[:, qo:qo + ql], ALU.add),
                reads=(r_tmpf, r_B), war=(r_lnb[i],), sets=(r_lnb[i],))
            b.stream_write(ln1_d[c * 128:(c + 1) * 128, qo:qo + ql], lnb[i][:, 0:ql],
                           r_lnb[i])

    def rope_combine(dst_ap, r_dst, src_ap, r_src, qo, ql):
        b.op("v", lambda e: e.tensor_tensor(rp1[:, 0:ql], src_ap, T1[:, qo:qo + ql],
                                            ALU.mult),
             reads=(r_src, d_cst), war=(r_rp1,), sets=(r_rp1,))
        b.op("v", lambda e: e.tensor_tensor(rp2[:, 0:ql], psD[:, 0:ql], T2[:, qo:qo + ql],
                                            ALU.mult),
             reads=(r_psD, d_cst), war=(r_rp2,), sets=(r_rp2,))
        b.op("v", lambda e: e.tensor_tensor(dst_ap, rp1[:, 0:ql], rp2[:, 0:ql], ALU.add),
             reads=(r_rp1, r_rp2), sets=(r_dst,))

    passes = [("dkv", wdkv, 0, NC_KV, BC_DKV)]
    for g0 in range(0, NC_D, 6):
        passes.append(("kr", wkr, g0, min(g0 + 6, NC_D), BC_KR))
    for g0 in range(0, NC_Q, 6):
        passes.append(("dq", wdq, g0, min(g0 + 6, NC_Q), BC_DQ))
    pshalf = [(psA, 0, r_psA[0]), (psA, 512, r_psA[1]), (psB, 0, r_psB[0]),
              (psB, 512, r_psB[1]), (psC, 0, r_psC[0]), (psC, 512, r_psC[1])]
    ibw = [0]
    for ip, (kind, wt, m0g, m1g, bc0) in enumerate(passes):
        nm = m1g - m0g
        for (qo, ql) in QCS:
            if ip == 0:
                p1_block(qo, ql)
            if ip == 1 and qo == 0:
                b.barrier_y(r_lnb)
            for c in range(NC_D):
                iw = ibw[0] % 4
                ibw[0] += 1
                b.dma("g", wb_p2[iw][:, 0:nm * 128].rearrange("p (m f) -> p m f", m=nm),
                      wt[c, m0g:m1g, :, :].rearrange("m p f -> p m f"), d_wbp2[iw])
                wdep = [("w", rl.wsem, rl.wcnt) for rl in r_lnb
                        if hasattr(rl, "wsem")]
                b._dep_wait("y", wdep, [])
                b.dma("y", lrhs[iw][:, 0:ql], ln1_d[c * 128:(c + 1) * 128, qo:qo + ql],
                      d_lrhs[iw])
                for m in range(nm):
                    ps, po, rps = pshalf[m]
                    b.op("t", lambda e, ps=ps, po=po, ql=ql, iw=iw, m=m, c=c: e.matmul(
                        ps[:, po:po + ql], wb_p2[iw][:, m * 128:(m + 1) * 128],
                        lrhs[iw][:, 0:ql], start=(c == 0), stop=(c == NC_D - 1)),
                        reads=(d_wbp2[iw], d_lrhs[iw]),
                        war=((rps,) if c == 0 else ()),
                        sets=((rps,) if c == NC_D - 1 else ()),
                        tick=(m == nm - 1 or c == NC_D - 1))
            for m in range(nm):
                ps, po, rps = pshalf[m]
                ma = m0g + m
                if kind == "dkv":
                    b.op("s", lambda e, ps=ps, po=po, ql=ql, ma=ma, qo=qo: e.activation(
                        cKV[:, ma * S + qo:ma * S + qo + ql], ps[:, po:po + ql],
                        AF.Identity, bias=bp[:, BC_DKV + ma:BC_DKV + ma + 1]),
                        reads=(rps, d_cst), war=(r_cKV,), sets=(r_cKV,))
                elif kind == "dq":
                    i = m % 2
                    b.op("s", lambda e, ps=ps, po=po, ql=ql, ma=ma, i=i: e.activation(
                        cqst[i][:, 0:ql], ps[:, po:po + ql], AF.Identity,
                        bias=bp[:, BC_DQ + ma:BC_DQ + ma + 1]),
                        reads=(rps, d_cst), war=(r_cqst[i],), sets=(r_cqst[i],))
                    b.stream_write(cq_d[ma * 128:(ma + 1) * 128, qo:qo + ql],
                                   cqst[i][:, 0:ql], r_cqst[i])
                else:
                    i = m % 2
                    b.op("s", lambda e, ps=ps, po=po, ql=ql, ma=ma, i=i: e.activation(
                        krt[i][:, 0:ql], ps[:, po:po + ql], AF.Identity,
                        bias=bp[:, BC_KR + ma:BC_KR + ma + 1]),
                        reads=(rps, d_cst), war=(r_krt[i],), sets=(r_krt[i],))
                    b.op("t", lambda e, ql=ql, i=i: e.matmul(
                        psD[:, 0:ql], pi_b[:, :], krt[i][:, 0:ql], start=True, stop=True),
                        reads=(r_krt[i], d_cst), war=(r_psD,), sets=(r_psD,))
                    rope_combine(kR[:, ma * S + qo:ma * S + qo + ql], r_kR,
                                 krt[i][:, 0:ql], r_krt[i], qo, ql)
    b.barrier_y(r_cqst)
    b.fence(stream_resses=(r_cqst[0], r_cqst[1]))

    NQ = len(QCS)
    r_qCq = [R(f"qC{q}") for q in range(NQ)]
    r_qRq = [R(f"qR{q}") for q in range(NQ)]

    def rope_combine_e(dst_ap, r_dst, src_ap, r_src, qo, ql):
        b.op("v", lambda e: e.tensor_tensor(rp1[:, 0:ql], src_ap, T1[:, qo:qo + ql],
                                            ALU.mult),
             reads=(r_src, d_cst), war=(r_rp1,), sets=(r_rp1,))
        b.op("v", lambda e: e.tensor_tensor(rp2[:, 0:ql], psE[:, 0:ql], T2[:, qo:qo + ql],
                                            ALU.mult),
             reads=(r_psE, d_cst), war=(r_rp2,), sets=(r_rp2,))
        b.op("v", lambda e: e.tensor_tensor(dst_ap, rp1[:, 0:ql], rp2[:, 0:ql], ALU.add),
             reads=(r_rp1, r_rp2), sets=(r_dst,))

    def prod_dmas(h):
        iw = h % 2
        b.dma("g", wuq_b[iw][:, :].rearrange("p (k f) -> p k f", k=NC_Q),
              wuq[:, h, :, :].rearrange("k p f -> p k f"), d_wuq[iw])
        b.dma("g", wukv_b[iw][:, 0:NC_KV * 128].rearrange("p (k f) -> p k f", k=NC_KV),
              wuk[:, h, :, :].rearrange("k p f -> p k f"), d_wukv[iw])
        b.dma("g", wukv_b[iw][:, NC_KV * 128:].rearrange("p (k f) -> p k f", k=NC_KV),
              wuv[:, h, :, :].rearrange("k p f -> p k f"), d_wukv[iw])

    def prefetch_cqr(qi):
        qo, ql = QCS[qi]
        ic = qi % 2
        b.dma("y", cqr[ic][:, 0:NC_Q * ql].rearrange("p (k f) -> p k f", k=NC_Q),
              cq_d.rearrange("(k p) t -> p k t", p=128)[:, :, qo:qo + ql], d_cqr[ic])

    def prod_qc(h, qi):
        iw = h % 2
        qo, ql = QCS[qi]
        ic = qi % 2
        for kc in range(NC_Q):
            b.op("t", lambda e, kc=kc, ql=ql, ic=ic, iw=iw: e.matmul(
                psE[:, 0:ql], wuq_b[iw][:, kc * 128:(kc + 1) * 128],
                cqr[ic][:, kc * ql:kc * ql + ql],
                start=(kc == 0), stop=(kc == NC_Q - 1)),
                reads=(d_wuq[iw], d_cqr[ic]),
                war=((r_psE,) if kc == 0 else ()),
                sets=((r_psE,) if kc == NC_Q - 1 else ()), tick=(kc == NC_Q - 1))
        b.op("s", lambda e, qo=qo, ql=ql, h=h: e.activation(
            qC_h[:, qo:qo + ql], psE[:, 0:ql], AF.Identity,
            bias=bp[:, BC_UQ + h:BC_UQ + h + 1]),
            reads=(r_psE, d_cst), war=(r_qCq[qi],), sets=(r_qCq[qi],))
        b.op("t", lambda e, qo=qo, ql=ql, qi=qi: e.matmul(
            psE[:, 0:ql], pi_b[:, :], qC_h[:, qo:qo + ql], start=True, stop=True),
            reads=(r_qCq[qi], d_cst), war=(r_psE,), sets=(r_psE,))
        rope_combine_e(qR_h[:, qo:qo + ql], r_qRq[qi], qC_h[:, qo:qo + ql],
                       r_qCq[qi], qo, ql)

    def prod_kv(h):
        iw = h % 2
        for (qo, ql) in QCS:
            for kc in range(NC_KV):
                b.op("t", lambda e, kc=kc, qo=qo, ql=ql, iw=iw: e.matmul(
                    psE[:, 0:ql], wukv_b[iw][:, kc * 128:(kc + 1) * 128],
                    cKV[:, kc * S + qo:kc * S + qo + ql],
                    start=(kc == 0), stop=(kc == NC_KV - 1)),
                    reads=(d_wukv[iw], r_cKV),
                    war=((r_psE,) if kc == 0 else ()),
                    sets=((r_psE,) if kc == NC_KV - 1 else ()), tick=(kc == NC_KV - 1))
            b.op("s", lambda e, qo=qo, ql=ql, h=h: e.activation(
                kC_h[:, qo:qo + ql], psE[:, 0:ql], AF.Identity,
                bias=bp[:, BC_UK + h:BC_UK + h + 1]),
                reads=(r_psE, d_cst), war=(r_kC,), sets=(r_kC,))
        ntt = S // 128
        for g0 in range(0, ntt, 4):
            gn = min(4, ntt - g0)
            for tj in range(gn):
                tt = g0 + tj
                for kc in range(NC_KV):
                    b.op("t", lambda e, tt=tt, tj=tj, kc=kc, iw=iw: e.matmul(
                        psE[:, tj * 128:(tj + 1) * 128],
                        cKV[:, kc * S + tt * 128:kc * S + (tt + 1) * 128],
                        wukv_b[iw][:, (NC_KV + kc) * 128:(NC_KV + kc + 1) * 128],
                        start=(kc == 0), stop=(kc == NC_KV - 1)),
                        reads=(d_wukv[iw], r_cKV),
                        war=((r_psE,) if (kc == 0 and tj == 0) else ()),
                        sets=((r_psE,) if (kc == NC_KV - 1 and tj == gn - 1) else ()),
                        tick=(kc == NC_KV - 1 and tj == gn - 1))
            b.op("s", lambda e, g0=g0, gn=gn: e.activation(
                v_h[:, g0 * 128:(g0 + gn) * 128], psE[:, 0:gn * 128], AF.Identity),
                reads=(r_psE,), war=(r_v,), sets=(r_v,))

    sslot = [(psC, 0, r_psC[0]), (psC, 512, r_psC[1]),
             (psB, 0, r_psB[0]), (psB, 512, r_psB[1])]

    def attn_qc(h, qi):
        qo, ql = QCS[qi]
        nkb = (qo + ql - 1) // 128 + 1
        LAG = min(2, nkb)

        def emit_scores(kb):
            ie = kb % 4
            ps_s, pco, rsc = sslot[ie]
            b.op("t", lambda e, kb=kb, qo=qo, ql=ql, ps_s=ps_s, pco=pco, qi=qi: e.matmul(
                ps_s[:, pco:pco + ql], kC_h[:, kb * 128:(kb + 1) * 128],
                qC_h[:, qo:qo + ql], start=True, stop=False),
                reads=(r_kC, r_qCq[qi]), war=(rsc,), tick=False)
            b.op("t", lambda e, kb=kb, qo=qo, ql=ql, ps_s=ps_s, pco=pco, h=h, qi=qi:
                 e.matmul(ps_s[:, pco:pco + ql],
                          kR[:, h * S + kb * 128:h * S + (kb + 1) * 128],
                          qR_h[:, qo:qo + ql], start=False, stop=True),
                 reads=(r_kR, r_qRq[qi]), sets=(rsc,))
            b.op("s", lambda e, ps_s=ps_s, pco=pco, ql=ql, ie=ie: e.activation(
                expb[ie][:, 0:ql], ps_s[:, pco:pco + ql], AF.Exp, scale=SM_SCALE),
                reads=(rsc,), war=(r_expb[ie],), sets=(r_expb[ie],))
            dcol = kb * 128 - qo
            if dcol >= 0:
                if dcol > 0:
                    b.op("v", lambda e, ie=ie, dcol=dcol: e.memset(
                        expb[ie][:, 0:dcol], 0.0),
                        reads=(r_expb[ie],), sets=(r_expb[ie],))
                hi = min(dcol + 128, ql)
                b.op("v", lambda e, ie=ie, dcol=dcol, hi=hi: e.tensor_tensor(
                    expb[ie][:, dcol:hi], expb[ie][:, dcol:hi],
                    tri_b[:, 0:hi - dcol], ALU.mult),
                    reads=(r_expb[ie], d_cst), sets=(r_expb[ie],))

        def emit_consume(kb):
            ie = kb % 4
            b.op("t", lambda e, ie=ie, ql=ql, kb=kb, nkb=nkb: e.matmul(
                psD[:, 0:ql], ao_b[:, :], expb[ie][:, 0:ql],
                start=(kb == 0), stop=(kb == nkb - 1)),
                reads=(r_expb[ie], d_cst),
                war=((r_psD,) if kb == 0 else ()),
                sets=((r_psD,) if kb == nkb - 1 else ()), tick=(kb == nkb - 1))
            b.op("t", lambda e, ie=ie, ql=ql, kb=kb, nkb=nkb: e.matmul(
                psA[:, 0:ql], v_h[:, kb * 128:(kb + 1) * 128], expb[ie][:, 0:ql],
                start=(kb == 0), stop=(kb == nkb - 1)),
                reads=(r_v, r_expb[ie]),
                war=((r_psA[0],) if kb == 0 else ()),
                sets=((r_psA[0],) if kb == nkb - 1 else ()), tick=True)

        for kb in range(nkb + LAG):
            if kb < nkb:
                emit_scores(kb)
            if kb >= LAG:
                emit_consume(kb - LAG)
        ix = qi % 2
        b.op("v", lambda e, ql=ql: e.reciprocal(rcp[:, 0:ql], psD[:, 0:ql]),
             reads=(r_psD,), war=(r_rcp,), sets=(r_rcp,))
        b.op("v", lambda e, ql=ql, ix=ix: e.tensor_tensor(
            cxst[ix][:, 0:ql], psA[:, 0:ql], rcp[:, 0:ql], ALU.mult),
            reads=(r_psA[0], r_rcp), war=(r_cxst[ix],), sets=(r_cxst[ix],))
        b.op("v", lambda e, ql=ql, ix=ix, h=h: e.tensor_scalar(
            cxst[ix][:, 0:ql], cxst[ix][:, 0:ql],
            bp[:, BC_UV + h:BC_UV + h + 1], None, ALU.add),
            reads=(r_cxst[ix], d_cst), sets=(r_cxst[ix],))
        b.stream_write(ctx_d[h * 128:(h + 1) * 128, qo:qo + ql],
                       cxst[ix][:, 0:ql], r_cxst[ix])

    prod_dmas(0)
    for qi in range(NQ):
        prefetch_cqr(qi)
        prod_qc(0, qi)
    prod_kv(0)
    for h in range(NH):
        if h + 1 < NH:
            prod_dmas(h + 1)
        for qi in range(NQ):
            if h + 1 < NH:
                prefetch_cqr(qi)
            attn_qc(h, qi)
            if h + 1 < NH:
                prod_qc(h + 1, qi)
        if h + 1 < NH:
            prod_kv(h + 1)
    b.barrier_y(r_cxst)
    b.fence(stream_resses=(r_cxst[0], r_cxst[1]))

    pj = [(psA, 0, r_psA[0]), (psA, 512, r_psA[1]), (psB, 0, r_psB[0]),
          (psB, 512, r_psB[1])]
    for (qo, ql) in QCS:
        cidx = [0]
        for jg in range(NC_D // 4):
            for hh in range(2):
                b.dma("g", wo_b[hh][:, :].rearrange("p (kj f) -> p kj f", kj=32),
                      wo[jg, hh * 8:(hh + 1) * 8, :, :, :]
                      .rearrange("k j p f -> p (k j) f"), d_wo[hh])
            for hd in range(NC_D):
                icx = hd % 4
                b.dma("y", cxbuf[icx][:, 0:ql],
                      ctx_d[hd * 128:(hd + 1) * 128, qo:qo + ql], d_cx[icx])
                hh, hdo = hd // 8, hd % 8
                for j in range(4):
                    ps, po, rps = pj[j]
                    b.op("t", lambda e, ps=ps, po=po, ql=ql, hh=hh, hdo=hdo, icx=icx,
                         j=j, hd=hd: e.matmul(
                             ps[:, po:po + ql],
                             wo_b[hh][:, (hdo * 4 + j) * 128:(hdo * 4 + j + 1) * 128],
                             cxbuf[icx][:, 0:ql],
                             start=(hd == 0), stop=(hd == NC_D - 1)),
                        reads=(d_wo[hh], d_cx[icx]),
                        war=((rps,) if hd == 0 else ()),
                        sets=((rps,) if hd == NC_D - 1 else ()),
                        tick=(j == 3 or hd == NC_D - 1))
            for j in range(4):
                ps, po, rps = pj[j]
                jj = jg * 4 + j
                ih = jj % 2
                b.dma("y", hbuf[ih][:, 0:ql], h_t[jj * 128:(jj + 1) * 128, qo:qo + ql],
                      d_hb[ih])
                if qo < HALF:
                    dst = x1[:, jj * HALF + qo:jj * HALF + qo + ql]
                    rdst = r_x1
                else:
                    ixs = jj % 2
                    dst = x1st[ixs][:, 0:ql]
                    rdst = r_x1st[ixs]
                b.op("v", lambda e, ps=ps, po=po, ql=ql, jj=jj, ih=ih, dst=dst:
                     e.scalar_tensor_tensor(dst, ps[:, po:po + ql],
                                            bp[:, BC_O + jj:BC_O + jj + 1],
                                            hbuf[ih][:, 0:ql], ALU.add, ALU.add),
                     reads=(rps, d_hb[ih], d_cst), war=(rdst,), sets=(rdst,))
                if qo >= HALF:
                    ixs = jj % 2
                    b.stream_write(
                        x1b_d[jj * 128:(jj + 1) * 128, qo - HALF:qo - HALF + ql],
                        x1st[ixs][:, 0:ql], r_x1st[ixs])
                    sfs, r_sfs = x1st[ixs][:, 0:ql], r_x1st[ixs]
                else:
                    sfs, r_sfs = dst, r_x1
                ci = cidx[0]
                cidx[0] += 1
                b.op("s", lambda e, sfs=sfs, ql=ql: e.activation(
                    xsq[:, 0:ql], sfs, AF.Square),
                    reads=(r_sfs,), war=(r_xsq,), sets=(r_xsq,))
                b.op("t", lambda e, sfs=sfs, ql=ql, ci=ci: e.matmul(
                    psC[:, 0:ql], ao_f[:, :], sfs,
                    start=(ci == 0), stop=(ci == NC_D - 1)),
                    reads=(r_sfs, d_cst),
                    war=((r_psC[0],) if ci == 0 else ()),
                    sets=((r_psC[0],) if ci == NC_D - 1 else ()), tick=(ci == NC_D - 1))
                b.op("t", lambda e, ql=ql, ci=ci: e.matmul(
                    psC[:, 512:512 + ql], ao_f[:, :], xsq[:, 0:ql],
                    start=(ci == 0), stop=(ci == NC_D - 1)),
                    reads=(r_xsq,),
                    war=((r_psC[1],) if ci == 0 else ()),
                    sets=((r_psC[1],) if ci == NC_D - 1 else ()), tick=True)
        stats_chain(qo, ql, psC, r_psC[0], psC[:, 512:1024], r_psC[1])

    def ln2_gate(t0):
        for c in range(NC_D):
            for (ho, hl) in _chunks(0, HALF, 1024):
                b.op("v", lambda e, c=c, ho=ho, hl=hl: e.tensor_tensor(
                    tmpf[:, 0:hl], x1[:, c * HALF + ho:c * HALF + ho + hl],
                    A_t[:, t0 + ho:t0 + ho + hl], ALU.mult),
                    reads=(r_x1, r_A), war=(r_tmpf,), sets=(r_tmpf,))
                b.op("v", lambda e, c=c, ho=ho, hl=hl: e.tensor_tensor(
                    ln2[:, c * HALF + ho:c * HALF + ho + hl], tmpf[:, 0:hl],
                    B_t[:, t0 + ho:t0 + ho + hl], ALU.add),
                    reads=(r_tmpf, r_B), war=(r_ln2,), sets=(r_ln2,))
        glog, sm, swk, msk, mx1, rs1 = g_sc
        g0, g1, g2, g3, g4, g5 = r_gsc
        for tt in range(HALF // 128):
            for c in range(NC_D):
                b.op("t", lambda e, tt=tt, c=c: e.matmul(
                    psD[:, 0:16], ln2[:, c * HALF + tt * 128:c * HALF + (tt + 1) * 128],
                    gw_b[:, c * 16:(c + 1) * 16],
                    start=(c == 0), stop=(c == NC_D - 1)),
                    reads=(r_ln2, d_gwc),
                    war=((r_psD,) if c == 0 else ()),
                    sets=((r_psD,) if c == NC_D - 1 else ()), tick=(c == NC_D - 1))
            b.op("v", lambda e: e.reduce_max(mx1[:, 0:1], psD[:, 0:16], axis=AX.X,
                                             negate=True),
                 reads=(r_psD,), war=(g4,), sets=(g4,))
            b.op("s", lambda e: e.activation(glog[:, :], psD[:, 0:16], AF.Exp,
                                             bias=mx1[:, 0:1]),
                 reads=(r_psD, g4), war=(g0,), sets=(g0,))
            b.op("v", lambda e: e.reduce_sum(rs1[:, 0:1], glog[:, :], axis=AX.X),
                 reads=(g0,), war=(g5,), sets=(g5,))
            b.op("v", lambda e: e.reciprocal(rs1[:, 0:1], rs1[:, 0:1]),
                 reads=(g5,), sets=(g5,))
            b.op("v", lambda e: e.tensor_scalar(sm[:, :], glog[:, :], rs1[:, 0:1], None,
                                                ALU.mult),
                 reads=(g0, g5), war=(g1,), sets=(g1,))
            b.op("v", lambda e: e.tensor_copy(swk[:, :], sm[:, :]),
                 reads=(g1,), war=(g2,), sets=(g2,))
            for it in range(6):
                b.op("v", lambda e: e.reduce_max(mx1[:, 0:1], swk[:, :], axis=AX.X),
                     reads=(g2,), war=(g4,), sets=(g4,))
                if it < 5:
                    b.op("v", lambda e: e.tensor_scalar(mski[:, :], swk[:, :], mx1[:, 0:1],
                                                        None, ALU.is_ge),
                         reads=(g2, g4), war=(g3,), sets=(g3,))
                    b.op("v", lambda e: e.select(swk[:, :], mski[:, :], negt[:, :],
                                                 swk[:, :], add_drain=True),
                         reads=(g3, r_cst, g2), sets=(g2,))
            b.op("v", lambda e: e.tensor_scalar(msk[:, :], sm[:, :], mx1[:, 0:1], None,
                                                ALU.is_ge),
                 reads=(g1, g4), war=(g3,), sets=(g3,))
            b.op("v", lambda e: e.tensor_tensor(swk[:, :], sm[:, :], msk[:, :], ALU.mult),
                 reads=(g1, g3), sets=(g2,))
            b.op("t", lambda e: e.transpose(psE[0:16, 0:128], swk[:, :], idn[:, :]),
                 reads=(g2, d_cst), war=(r_psE,), sets=(r_psE,))
            b.op("v", lambda e, tt=tt: e.tensor_copy(
                cwT[0:16, t0 + tt * 128:t0 + (tt + 1) * 128], psE[0:16, 0:128]),
                reads=(r_psE,), war=(r_cwT,), sets=(r_cwT,))

    def ffn_half(t0):
        TCH = _chunks(0, HALF, 512)
        for ex in range(E):
            for ci, (co, cl) in enumerate(TCH):
                ps, rps = (psD, r_psD) if ci % 2 == 0 else (psE, r_psE)
                b.op("t", lambda e, ps=ps, co=co, cl=cl, ex=ex: e.matmul(
                    ps[:, 0:cl], selm[0:16, ex * 128:(ex + 1) * 128],
                    cwT[0:16, t0 + co:t0 + co + cl],
                    start=True, stop=True),
                    reads=(r_cwT, d_cst), war=(rps,), sets=(rps,))
                b.op("s", lambda e, ps=ps, co=co, cl=cl: e.activation(
                    cwb[:, co:co + cl], ps[:, 0:cl], AF.Identity),
                    reads=(rps,), war=(r_cwb,), sets=(r_cwb,))
            for m in range(NC_M):
                img = (ex * NC_M + m) % 2
                b.dma("g", wgb[img][:, :].rearrange("p (k f) -> p k f", k=NC_D),
                      wg[ex, m, :, :, :].rearrange("k p f -> p k f"), d_wgb[img])
                b.dma("g", wub[img][:, :].rearrange("p (k f) -> p k f", k=NC_D),
                      wu[ex, m, :, :, :].rearrange("k p f -> p k f"), d_wub[img])
                for k in range(NC_D):
                    for ci, (co, cl) in enumerate(TCH):
                        b.op("t", lambda e, k=k, co=co, cl=cl, ci=ci, img=img: e.matmul(
                            psA[:, ci * 512:ci * 512 + cl],
                            wgb[img][:, k * 128:(k + 1) * 128],
                            ln2[:, k * HALF + co:k * HALF + co + cl],
                            start=(k == 0), stop=(k == NC_D - 1)),
                            reads=(d_wgb[img], r_ln2),
                            war=((r_psA[ci],) if k == 0 else ()),
                            sets=((r_psA[ci],) if k == NC_D - 1 else ()),
                            tick=(k == NC_D - 1))
                for k in range(NC_D):
                    for ci, (co, cl) in enumerate(TCH):
                        b.op("t", lambda e, k=k, co=co, cl=cl, ci=ci, img=img: e.matmul(
                            psB[:, ci * 512:ci * 512 + cl],
                            wub[img][:, k * 128:(k + 1) * 128],
                            ln2[:, k * HALF + co:k * HALF + co + cl],
                            start=(k == 0), stop=(k == NC_D - 1)),
                            reads=(d_wub[img], r_ln2),
                            war=((r_psB[ci],) if k == 0 else ()),
                            sets=((r_psB[ci],) if k == NC_D - 1 else ()),
                            tick=(k == NC_D - 1))
                rA = (r_psA[0], r_psA[1]) if len(TCH) > 1 else (r_psA[0],)
                rB = (r_psB[0], r_psB[1]) if len(TCH) > 1 else (r_psB[0],)
                b.op("s", lambda e: e.activation(gs_t[:, 0:HALF], psA[:, 0:HALF],
                                                 AF.Silu),
                     reads=rA, war=(r_gs,), sets=(r_gs,))
                b.op("v", lambda e: e.scalar_tensor_tensor(
                    us_t[:, 0:HALF], psB[:, 0:HALF], 1.0, cwb[:, 0:HALF],
                    ALU.mult, ALU.mult),
                    reads=rB + (r_cwb,), war=(r_us,), sets=(r_us,))
                b.op("v", lambda e, m=m: e.tensor_tensor(
                    t_e[:, m * HALF:(m + 1) * HALF], gs_t[:, 0:HALF], us_t[:, 0:HALF],
                    ALU.mult),
                    reads=(r_gs, r_us), war=(r_te,), sets=(r_te,))
            for j in range(NC_D):
                ij = (ex * NC_D + j) % 2
                b.dma("g", wdb[ij][:, :].rearrange("p (m f) -> p m f", m=NC_M),
                      wd[ex, j, :, :, :].rearrange("m p f -> p m f"), d_wdb[ij])
                for m in range(NC_M):
                    for ci, (co, cl) in enumerate(TCH):
                        b.op("t", lambda e, m=m, co=co, cl=cl, ci=ci, ij=ij: e.matmul(
                            psC[:, ci * 512:ci * 512 + cl],
                            wdb[ij][:, m * 128:(m + 1) * 128],
                            t_e[:, m * HALF + co:m * HALF + co + cl],
                            start=(m == 0), stop=(m == NC_M - 1)),
                            reads=(d_wdb[ij], r_te),
                            war=((r_psC[ci],) if m == 0 else ()),
                            sets=((r_psC[ci],) if m == NC_M - 1 else ()),
                            tick=(m == NC_M - 1))
                for ci, (co, cl) in enumerate(TCH):
                    b.op("v", lambda e, j=j, co=co, cl=cl, ci=ci: e.tensor_tensor(
                        x1[:, j * HALF + co:j * HALF + co + cl],
                        x1[:, j * HALF + co:j * HALF + co + cl],
                        psC[:, ci * 512:ci * 512 + cl], ALU.add),
                        reads=(r_psC[ci], r_x1), sets=(r_x1,))
        for c in range(NC_D):
            b.stream_write(out_t[c * 128:(c + 1) * 128, t0:t0 + HALF],
                           x1[:, c * HALF:(c + 1) * HALF], r_x1)

    b.fence(stream_resses=(r_x1st[0], r_x1st[1]))
    ln2_gate(0)
    ffn_half(0)
    b.barrier_y(r_x1st)
    b.barrier_y([r_x1])
    for c in range(NC_D):
        i = c % 2
        b.dma("y", x1[:, c * HALF:(c + 1) * HALF], x1b_d[c * 128:(c + 1) * 128, :],
              d_x1r[i], reads=(r_x1,))
    r_x1.producer = ("y", d_x1r[1].dmasem, d_x1r[1].dmacnt)
    r_x1.readers = [("y", d_x1r[0].dmasem, d_x1r[0].dmacnt)]
    r_x1_extra = Res("x1extra")
    r_x1_extra.producer = ("y", d_x1r[0].dmasem, d_x1r[0].dmacnt)
    b.op("v", lambda e: e.engine_nop(), reads=(r_x1_extra,))
    b.fence(stream_resses=(r_x1,))
    ln2_gate(HALF)
    ffn_half(HALF)
    b.barrier_y([r_x1])

    with nc.Block() as block:
        @block.tensor
        def _(e):
            for f in b.q["t"]:
                f(e)

        @block.scalar
        def _(e):
            for f in b.q["s"]:
                f(e)

        @block.vector
        def _(e):
            for f in b.q["v"]:
                f(e)

        @block.gpsimd
        def _(e):
            for f in b.q["g"]:
                f(e)

        @block.sync
        def _(e):
            for f in b.q["y"]:
                f(e)

    b.close_sems()
    return nc



def _eo_perm():
    p = []
    for hh in range(NH):
        base = hh * HDIM
        p += list(range(base, base + HDIM, 2)) + list(range(base + 1, base + HDIM, 2))
    return np.array(p)


def _tile_km(w, nk, nm):
    return np.ascontiguousarray(w.reshape(nk, 128, nm, 128).transpose(0, 2, 1, 3))


def host_prep(ins, S):
    perm = _eo_perm()
    d = {}
    d["wdkv"] = _tile_km(ins["L0_attn_W_DKV"], NC_D, NC_KV)
    d["wuk"] = _tile_km(ins["L0_attn_W_UK"][:, perm], NC_KV, NH)
    d["wuv"] = _tile_km(ins["L0_attn_W_UV"], NC_KV, NH)
    d["wdq"] = _tile_km(ins["L0_attn_W_DQ"], NC_D, NC_Q)
    d["wuq"] = _tile_km(ins["L0_attn_W_UQ"][:, perm], NC_Q, NH)
    d["wkr"] = _tile_km(ins["L0_attn_W_KR"][:, perm], NC_D, NC_D)
    wot = _tile_km(ins["L0_attn_W_O"], NC_D, NC_D)
    d["wo"] = np.ascontiguousarray(
        wot.reshape(NC_D, NC_D // 4, 4, 128, 128).transpose(1, 0, 2, 3, 4))
    d["gw"] = np.ascontiguousarray(ins["L0_ffn_gate_w"].T.reshape(NC_D, 128, 16))
    wgp = ins["L0_ffn_w_gate"].transpose(0, 2, 1)
    wup = ins["L0_ffn_w_up"].transpose(0, 2, 1)
    wdp = ins["L0_ffn_w_down"].transpose(0, 2, 1)
    d["wg"] = np.ascontiguousarray(
        wgp.reshape(E, NC_D, 128, NC_M, 128).transpose(0, 3, 1, 2, 4))
    d["wu"] = np.ascontiguousarray(
        wup.reshape(E, NC_D, 128, NC_M, 128).transpose(0, 3, 1, 2, 4))
    d["wd"] = np.ascontiguousarray(
        wdp.reshape(E, NC_M, 128, NC_D, 128).transpose(0, 3, 1, 2, 4))
    bpk = np.zeros((128, 98), dtype=np.float32)
    bpk[:, 97] = 1.0

    def put(col, vec):
        n = len(vec) // 128
        bpk[:, col:col + n] = vec.reshape(n, 128).T
    put(0, ins["L0_attn_b_DKV"])
    put(4, ins["L0_attn_b_UK"][perm])
    put(20, ins["L0_attn_b_UV"])
    put(36, ins["L0_attn_b_DQ"])
    put(48, ins["L0_attn_b_UQ"][perm])
    put(64, ins["L0_attn_b_KR"][perm])
    put(80, ins["L0_attn_b_O"])
    d["biasp"] = bpk
    pos = np.arange(S, dtype=np.float64)
    invf = 1.0 / (10000.0 ** (np.arange(64, dtype=np.float64) * 2.0 / 128))
    ang = invf[:, None] * pos[None, :]
    cosd, sind = np.cos(ang), np.sin(ang)
    d["t1d"] = np.concatenate([cosd, cosd], axis=0).astype(np.float32)
    d["t2d"] = np.concatenate([-sind, sind], axis=0).astype(np.float32)
    d["cst_ao_f"] = np.ones((128, 128), dtype=np.float32)
    d["cst_ao_b"] = np.ones((128, 128), dtype=BF)
    d["cst_id"] = np.eye(128, dtype=np.float32)
    pim = np.zeros((128, 128), dtype=np.float32)
    for m in range(128):
        pim[(m + 64) % 128, m] = 1.0
    d["cst_pi"] = pim.astype(BF)
    tri = (np.arange(128)[:, None] <= np.arange(128)[None, :]).astype(np.float32)
    d["cst_tri"] = tri.astype(BF)
    d["cst_o1b"] = np.ones((1, 128), dtype=BF)
    sel = np.zeros((16, 16 * 128), dtype=np.float32)
    for ex in range(16):
        sel[ex, ex * 128:(ex + 1) * 128] = 1.0
    d["cst_sel"] = sel.astype(BF)
    return d


def kernel(**inputs):
    S = 2048
    ncore = 8
    shared = host_prep(inputs, S)
    h = inputs["L0_attn_h"]
    nc = build_program(S)
    in_maps = []
    for c in range(ncore):
        m = dict(shared)
        m["h_t"] = np.ascontiguousarray(h[c].T)
        in_maps.append(m)
    res = run_bass_kernel_spmd(nc, in_maps, core_ids=list(range(ncore)))
    out = np.empty((ncore, S, D), dtype=np.float32)
    for c in range(ncore):
        out[c] = res.results[c]["out_t"].T
    return out


# version 14
# speedup vs baseline: 1.0494x; 1.0257x over previous
import numpy as np
import ml_dtypes
import concourse.bass as bass
import concourse.mybir as mybir
from concourse.bass_utils import run_bass_kernel_spmd

F32 = mybir.dt.float32
BF16 = mybir.dt.bfloat16
AF = mybir.ActivationFunctionType
ALU = mybir.AluOpType
AX = mybir.AxisListType
BF = ml_dtypes.bfloat16

D = 2048
NH = 16
HDIM = 128
DKV = 512
DQ = 1536
E = 16
MD = 1408
NC_D = D // 128
NC_KV = DKV // 128
NC_Q = DQ // 128
NC_M = MD // 128
EPS = 1e-5
SM_SCALE = 1.0 / 16.0

ENGS = ("t", "s", "v", "g", "y")


def _chunks(lo, hi, w):
    out = []
    while lo < hi:
        out.append((lo, min(w, hi - lo)))
        lo += w
    return out


class Res:

    def __init__(self, name):
        self.name = name
        self.producer = None
        self.readers = []


class Build:
    def __init__(self, nc):
        self.nc = nc
        self.q = {k: [] for k in ENGS}
        self.sems = {}
        self.pn = {k: 0 for k in ("t", "s", "v")}
        self.psem = {}
        self.waited = {k: {} for k in ENGS}
        self.clean = {k: 0 for k in ENGS}
        self.epoch = {k: 0 for k in ENGS}
        self._sem_ctxs = []
        self.pend = {}
        self.default_reads = ()
        for e in ("t", "s", "v"):
            self.psem[e] = self.sem("p_" + e)

    def sem(self, name):
        cm = self.nc.semaphore(name)
        h = cm.__enter__()
        self._sem_ctxs.append(cm)
        self.sems[name] = h
        return h

    def close_sems(self):
        for cm in self._sem_ctxs:
            cm.__exit__(None, None, None)

    def _dep_wait(self, eng, raw_deps, war_deps):
        need_drain = False
        waits = []
        for (se, sem, val) in raw_deps:
            if se == eng:
                need_drain = True
            else:
                key = id(sem)
                if self.waited[eng].get(key, -1) < val:
                    self.waited[eng][key] = val
                    waits.append((sem, val))
        for (se, sem, val) in war_deps:
            if se == eng:
                need_drain = True
                continue
            key = id(sem)
            if self.waited[eng].get(key, -1) < val:
                self.waited[eng][key] = val
                waits.append((sem, val))
        if need_drain and self.clean[eng] < self.epoch[eng]:
            self.q[eng].append(lambda e: e.drain())
            self.clean[eng] = self.epoch[eng]
        for (sem, val) in waits:
            self.q[eng].append(lambda e, s=sem, v=val: e.wait_ge(s, v))

    def op(self, eng, fn, reads=(), war=(), sets=(), tick=True):
        raw_deps = []
        war_deps = []
        reads = tuple(reads) + getattr(self, "default_reads", ())
        for r in reads:
            if r.producer:
                raw_deps.append(r.producer)
        for w in war:
            war_deps.extend(w.readers)
            if w.producer and not w.readers:
                war_deps.append(w.producer)
        self._dep_wait(eng, raw_deps, war_deps)
        self.epoch[eng] += 1
        if not tick:
            assert not sets
            self.q[eng].append(lambda e, f=fn: f(e))
            self.pend.setdefault(eng, []).extend(reads)
            return
        self.pn[eng] += 1
        val = self.pn[eng]
        sem = self.psem[eng]
        self.q[eng].append(lambda e, f=fn, s=sem: f(e).then_inc(s, 1))
        stamp = (eng, sem, val)
        for w in war:
            w.readers = []
        for s_ in sets:
            s_.producer = stamp
            s_.readers = []
        for r in reads:
            r.readers.append(stamp)
        for r in self.pend.get(eng, ()):
            r.readers.append(stamp)
        self.pend[eng] = []

    def mkbuf(self, name):
        r = Res(name)
        r.dmasem = self.sem("d_" + name)
        r.dmacnt = 0
        return r

    def dma(self, eng, out_ap, in_ap, buf, reads=()):
        raw_deps = []
        for r in reads:
            if r.producer:
                raw_deps.append(r.producer)
        war_deps = list(buf.readers)
        if buf.producer and not buf.readers:
            pe_, sem_, val_ = buf.producer
            key = id(sem_)
            if self.waited[eng].get(key, -1) < val_:
                self.waited[eng][key] = val_
                self.q[eng].append(lambda e, s=sem_, v=val_: e.wait_ge(s, v))
        self._dep_wait(eng, raw_deps, war_deps)
        self.epoch[eng] += 1
        buf.dmacnt += 16
        cnt = buf.dmacnt
        self.q[eng].append(lambda e, o=out_ap, i=in_ap, s=buf.dmasem:
                           e.dma_start(out=o, in_=i).then_inc(s, 16))
        buf.producer = (eng, buf.dmasem, cnt)
        buf.readers = []

    def stream_write(self, out_ap, in_ap, src_res):
        if not hasattr(src_res, "wsem"):
            src_res.wsem = self.sem("w_" + src_res.name)
            src_res.wcnt = 0
        raw_deps = [src_res.producer] if src_res.producer else []
        self._dep_wait("y", raw_deps, [])
        self.epoch["y"] += 1
        src_res.wcnt += 16
        self.q["y"].append(lambda e, o=out_ap, i=in_ap, s=src_res.wsem:
                           e.dma_start(out=o, in_=i).then_inc(s, 16))
        src_res.readers.append(("y", src_res.wsem, src_res.wcnt))

    def fence(self, stream_resses=()):
        vals = {e: self.pn[e] for e in ("t", "s", "v")}
        wdeps = []
        for r in stream_resses:
            if hasattr(r, "wsem"):
                wdeps.append(("y", r.wsem, r.wcnt))
        for eng in ENGS:
            deps = [(oe, self.psem[oe], vals[oe])
                    for oe in ("t", "s", "v") if oe != eng and vals[oe] > 0]
            deps += [d for d in wdeps if d[0] != eng or eng != "y"]
            if eng == "y":
                deps = [d for d in deps if d[1] is not None]
            self._dep_wait(eng, deps, [])
            if eng in ("t", "s", "v"):
                self.q[eng].append(lambda e: e.drain())
                self.clean[eng] = self.epoch[eng]
                self.epoch[eng] += 1

    def barrier_y(self, res_list):
        for r in res_list:
            if hasattr(r, "wsem"):
                self.q["y"].append(
                    lambda e, s=r.wsem, v=r.wcnt: e.wait_ge(s, v))


def build_program(S):
    HALF = S // 2
    QCS = _chunks(0, S, min(512, S // 2))
    nc = bass.Bass("TRN2", target_bir_lowering=False)

    dt = nc.dram_tensor
    h_t = dt("h_t", [D, S], F32, kind="ExternalInput").ap()
    wdkv = dt("wdkv", [NC_D, NC_KV, 128, 128], F32, kind="ExternalInput").ap()
    wuk = dt("wuk", [NC_KV, NH, 128, 128], F32, kind="ExternalInput").ap()
    wuv = dt("wuv", [NC_KV, NH, 128, 128], F32, kind="ExternalInput").ap()
    wdq = dt("wdq", [NC_D, NC_Q, 128, 128], F32, kind="ExternalInput").ap()
    wuq = dt("wuq", [NC_Q, NH, 128, 128], F32, kind="ExternalInput").ap()
    wkr = dt("wkr", [NC_D, NC_D, 128, 128], F32, kind="ExternalInput").ap()
    wo = dt("wo", [NC_D // 4, NC_D, 4, 128, 128], F32, kind="ExternalInput").ap()
    gw = dt("gw", [NC_D, 128, 16], F32, kind="ExternalInput").ap()
    wg = dt("wg", [E, NC_M, NC_D, 128, 128], F32, kind="ExternalInput").ap()
    wu = dt("wu", [E, NC_M, NC_D, 128, 128], F32, kind="ExternalInput").ap()
    wd = dt("wd", [E, NC_D, NC_M, 128, 128], F32, kind="ExternalInput").ap()
    biasp = dt("biasp", [128, 98], F32, kind="ExternalInput").ap()
    t1d = dt("t1d", [128, S], F32, kind="ExternalInput").ap()
    t2d = dt("t2d", [128, S], F32, kind="ExternalInput").ap()
    cst_ao_f = dt("cst_ao_f", [128, 128], F32, kind="ExternalInput").ap()
    cst_ao_b = dt("cst_ao_b", [128, 128], BF16, kind="ExternalInput").ap()
    cst_id = dt("cst_id", [128, 128], F32, kind="ExternalInput").ap()
    cst_pi = dt("cst_pi", [128, 128], BF16, kind="ExternalInput").ap()
    cst_tri = dt("cst_tri", [128, 128], BF16, kind="ExternalInput").ap()
    cst_o1b = dt("cst_o1b", [1, 128], BF16, kind="ExternalInput").ap()
    cst_sel = dt("cst_sel", [16, 16 * 128], BF16, kind="ExternalInput").ap()
    out_t = dt("out_t", [D, S], F32, kind="ExternalOutput").ap()

    ln1_d = dt("ln1_d", [D, S], BF16, kind="Internal").ap()
    cq_d = dt("cq_d", [DQ, S], BF16, kind="Internal").ap()
    ctx_d = dt("ctx_d", [D, S], BF16, kind="Internal").ap()
    x1b_d = dt("x1b_d", [D, HALF], F32, kind="Internal").ap()

    BC_DKV, BC_UK, BC_UV, BC_DQ, BC_UQ, BC_KR, BC_O = 0, 4, 20, 36, 48, 64, 80

    SB_BASE = 16384 + 1024
    off = [SB_BASE]

    def sb(name, shape, dtv, at=None):
        nbytes = int(np.prod(shape[1:])) * (4 if dtv == F32 else 2)
        if at is None:
            at = off[0]
            off[0] = at + ((nbytes + 63) // 64) * 64
        return nc.alloc_sbuf_tensor_at(name, shape, dtv, offset=at).ap()

    A_t = sb("A_t", [128, S], F32)
    B_t = sb("B_t", [128, S], F32)
    cwT = sb("cwT", [16, S], BF16)
    ao_f = sb("ao_f", [128, 128], F32)
    ao_b = sb("ao_b", [128, 128], BF16)
    idn = sb("idn", [128, 128], F32)
    pi_b = sb("pi_b", [128, 128], BF16)
    tri_b = sb("tri_b", [128, 128], BF16)
    o1b = sb("o1b", [1, 128], BF16)
    selm = sb("selm", [16, 16 * 128], BF16)
    bp = sb("bp", [128, 98], F32)
    nc.const_aps.register(F32, 0.0, bp[:, 96:97])
    nc.const_aps.register(F32, 1.0, bp[:, 97:98])
    gw_b = sb("gw_b", [128, 16 * 16], BF16)
    g_sc = [sb(f"g_sc{i}", [128, 16], F32) for i in range(6)]
    swkp = sb("swkp", [128, 8 * 16], F32)
    mski = sb("mski", [128, 16], mybir.dt.int32)
    negt = sb("negt", [128, 16], F32)

    REG1 = off[0]
    kR = sb("kR", [128, NC_D * S], BF16, at=REG1)
    x1 = sb("x1", [128, NC_D * HALF], F32, at=REG1)
    T1 = sb("T1", [128, S], F32, at=REG1 + NC_D * S * 2)
    T2 = sb("T2", [128, S], F32, at=REG1 + NC_D * S * 2 + S * 4)
    p4o = REG1 + NC_D * S * 2
    hbuf = [sb(f"hbuf{i}", [128, 512], F32, at=p4o + i * 2048) for i in range(2)]
    cxbuf = [sb(f"cxbuf{i}", [128, 512], BF16,
                at=p4o + (4096 + i * 1024 if i < 2 else 12288 + (i - 2) * 1024))
             for i in range(4)]
    x1st = [sb(f"x1st{i}", [128, 512], F32, at=p4o + 6144 + i * 2048) for i in range(2)]
    xsq = sb("xsq", [128, 512], F32, at=p4o + 10240)
    off[0] = REG1 + max(NC_D * S * 2 + S * 8, NC_D * HALF * 4 + 14336)

    REG2 = off[0]
    ln2 = sb("ln2", [128, NC_D * HALF], BF16, at=REG2)
    gs_t = sb("gs_t", [128, HALF], BF16, at=REG2 + NC_D * HALF * 2)
    us_t = sb("us_t", [128, HALF], BF16, at=REG2 + NC_D * HALF * 2 + HALF * 2)
    cwb = sb("cwb", [128, HALF], BF16, at=REG2 + NC_D * HALF * 2 + HALF * 4)
    cqst = [sb(f"cqst{i}", [128, 512], BF16, at=REG2 + i * 1024) for i in range(2)]
    krt = [sb(f"krt{i}", [128, 512], BF16, at=REG2 + 2048 + i * 1024) for i in range(2)]
    rp1 = sb("rp1", [128, 512], BF16, at=REG2 + 4096)
    rp2 = sb("rp2", [128, 512], BF16, at=REG2 + 5120)
    cqr = [sb(f"cqr{i}", [128, NC_Q * 512], BF16, at=REG2 + 6144 + i * NC_Q * 1024)
           for i in range(2)]
    off[0] = REG2 + max(NC_D * HALF * 2 + HALF * 6, 6144 + 2 * NC_Q * 1024)

    REG3 = off[0]
    cKV = sb("cKV", [128, NC_KV * S], BF16, at=REG3)
    t_e = sb("t_e", [128, NC_M * HALF], BF16, at=REG3)
    st_sc = [sb(f"st_sc{i}", [128, 512], F32, at=REG3 + i * 2048) for i in range(4)]
    off[0] = REG3 + max(NC_KV * S * 2 + 2048, NC_M * HALF * 2, 4 * 2048)

    REG4 = off[0]
    xb = [sb(f"xb{i}", [128, 1024], F32, at=REG4 + i * 4096) for i in range(2)]
    lnb = [sb(f"lnb{i}", [128, 1024], BF16, at=REG4 + 8192 + i * 2048) for i in range(2)]
    tmpf = sb("tmpf", [128, 1024], F32, at=REG4 + 12288)
    qC_h = sb("qC_h", [128, S], BF16, at=REG4)
    qR_h = sb("qR_h", [128, S], BF16, at=REG4 + S * 2)
    kC_h = sb("kC_h", [128, S], BF16, at=REG4 + S * 4)
    v_h = sb("v_h", [128, S], BF16, at=REG4 + S * 6)
    expb = [sb(f"expb{i}", [128, 512], BF16, at=REG4 + S * 8 + i * 1024) for i in range(4)]
    cxst = [sb(f"cxst{i}", [128, 512], BF16, at=REG4 + S * 8 + 4096 + i * 1024)
            for i in range(2)]
    rcp = sb("rcp", [128, 512], F32, at=REG3 + NC_KV * S * 2)
    wdb = [sb(f"wdb{i}", [128, NC_M * 128], BF16, at=REG4 + i * NC_M * 256) for i in range(2)]
    off[0] = REG4 + max(16384, S * 8 + 6144, 2 * NC_M * 256)

    REG5 = off[0]
    wb_p2 = [sb(f"wb_p2{i}", [128, 6 * 128], BF16, at=REG5 + i * 1536) for i in range(4)]
    lrhs = [sb(f"lrhs{i}", [128, 512], BF16, at=REG5 + 6144 + i * 1024) for i in range(4)]
    wuq_b = [sb(f"wuq_b{i}", [128, NC_Q * 128], BF16, at=REG5 + i * NC_Q * 256)
             for i in range(2)]
    wukv_b = [sb(f"wukv_b{i}", [128, 2 * NC_KV * 128], BF16,
                 at=REG5 + 2 * NC_Q * 256 + i * 2 * NC_KV * 256) for i in range(2)]
    wo_b = [sb(f"wo_b{i}", [128, 8 * 4 * 128], BF16, at=REG5 + i * 8 * 4 * 256)
            for i in range(2)]
    wgb = [sb(f"wgb{i}", [128, NC_D * 128], BF16, at=REG5 + i * NC_D * 256) for i in range(2)]
    wub = [sb(f"wub{i}", [128, NC_D * 128], BF16, at=REG5 + 2 * NC_D * 256 + i * NC_D * 256)
           for i in range(2)]
    off[0] = REG5 + max(5120, 2 * NC_Q * 256 + 4 * NC_KV * 256, 2 * 8 * 4 * 256,
                        4 * NC_D * 256)
    assert off[0] <= 229376, f"SBUF over budget: {off[0]} B/partition"

    psA = nc.alloc_psum_tensor("psA", [128, 1024], F32).ap()
    psB = nc.alloc_psum_tensor("psB", [128, 1024], F32).ap()
    psC = nc.alloc_psum_tensor("psC", [128, 1024], F32).ap()
    psD = nc.alloc_psum_tensor("psD", [128, 512], F32).ap()
    psE = nc.alloc_psum_tensor("psE", [128, 512], F32).ap()

    b = Build(nc)
    R = lambda n: Res(n)
    r_psA = [R("psA0"), R("psA1")]
    r_psB = [R("psB0"), R("psB1")]
    r_psC = [R("psC0"), R("psC1")]
    r_psD, r_psE = R("psD"), R("psE")
    r_A, r_B = R("A"), R("B")
    r_cKV, r_kR = R("cKV"), R("kR")
    r_kC, r_v = R("kC"), R("v")
    r_x1, r_te, r_cwT, r_cwb = R("x1"), R("te"), R("cwT"), R("cwb")
    r_ln2c = [R(f"ln2c{i}") for i in range(NC_D)]
    r_swkp = [R(f"swkp{i}") for i in range(8)]
    r_gs, r_us = R("gs"), R("us")
    r_expb = [R(f"expb{i}") for i in range(4)]
    r_rcp = R("rcp")
    r_cxst = [R("cxst0"), R("cxst1")]
    r_krt = [R("krt0"), R("krt1")]
    r_rp1, r_rp2 = R("rp1"), R("rp2")
    r_cqst = [R("cqst0"), R("cqst1")]
    r_x1st = [R("x1st0"), R("x1st1")]
    r_xsq = R("xsq")
    r_stsc = [R(f"stsc{i}") for i in range(4)]
    r_gsc = [R(f"gsc{i}") for i in range(6)]
    r_tmpf = R("tmpf")
    r_lnb = [R("lnb0"), R("lnb1")]
    r_cst = R("consts")

    d_xb = [b.mkbuf("xb0"), b.mkbuf("xb1")]
    d_lrhs = [b.mkbuf(f"lrhs{i}") for i in range(4)]
    d_wbp2 = [b.mkbuf(f"wbp2{i}") for i in range(4)]
    d_cqr = [b.mkbuf("cqr0"), b.mkbuf("cqr1")]
    d_wuq = [b.mkbuf("wuq0"), b.mkbuf("wuq1")]
    d_wukv = [b.mkbuf("wukv0"), b.mkbuf("wukv1")]
    d_wo = [b.mkbuf("wo0"), b.mkbuf("wo1")]
    d_hb = [b.mkbuf("hb0"), b.mkbuf("hb1")]
    d_cx = [b.mkbuf(f"cx{i}") for i in range(4)]
    d_wgb = [b.mkbuf("wgb0"), b.mkbuf("wgb1")]
    d_wub = [b.mkbuf("wub0"), b.mkbuf("wub1")]
    d_wdb = [b.mkbuf("wdb0"), b.mkbuf("wdb1")]
    d_cst = b.mkbuf("cst")
    d_gwc = b.mkbuf("gwc")
    d_x1r = [b.mkbuf("x1r0"), b.mkbuf("x1r1")]


    for dst, src in [(bp, biasp), (ao_f, cst_ao_f), (ao_b, cst_ao_b), (idn, cst_id),
                     (pi_b, cst_pi), (tri_b, cst_tri), (o1b, cst_o1b),
                     (selm, cst_sel), (T1, t1d), (T2, t2d)]:
        b.dma("y", dst[:, :], src[:, :], d_cst)
    for c in range(NC_D):
        b.dma("g", gw_b[:, c * 16:(c + 1) * 16], gw[c, :, :], d_gwc)
    b.op("v", lambda e: e.memset(negt[:, :], -1e30), sets=(r_cst,))
    b.default_reads = (d_cst,)

    def stats_chain(sl, slen, ps_sum_ap, r_sum, ps_sq_ap, r_sq):
        mt, e2, ve, sc3 = st_sc
        m0, m1, m2, m3 = r_stsc
        SL = slice(sl, sl + slen)
        L = slice(0, slen)
        b.op("v", lambda e: e.tensor_scalar(mt[:, L], ps_sum_ap[:, L], 1.0 / D, None,
                                            ALU.mult),
             reads=(r_sum,), war=(m0,), sets=(m0,))
        b.op("v", lambda e: e.tensor_scalar(e2[:, L], ps_sq_ap[:, L], 1.0 / D, None,
                                            ALU.mult),
             reads=(r_sq,), war=(m1,), sets=(m1,))
        b.op("v", lambda e: e.scalar_tensor_tensor(ve[:, L], mt[:, L], -1.0, mt[:, L],
                                                   ALU.mult, ALU.mult),
             reads=(m0,), war=(m2,), sets=(m2,))
        b.op("v", lambda e: e.tensor_tensor(ve[:, L], ve[:, L], e2[:, L], ALU.add),
             reads=(m2, m1), sets=(m2,))
        b.op("v", lambda e: e.tensor_scalar(ve[:, L], ve[:, L], EPS, None, ALU.add),
             reads=(m2,), sets=(m2,))
        b.op("s", lambda e: e.activation(sc3[:, L], ve[:, L], AF.Sqrt),
             reads=(m2,), war=(m3,), sets=(m3,))
        b.op("v", lambda e: e.reciprocal(sc3[:, L], sc3[:, L]),
             reads=(m3,), sets=(m3,))
        b.op("v", lambda e: e.tensor_tensor(A_t[:, SL], sc3[:, L], sc3[:, L], ALU.mult),
             reads=(m3,), war=(r_A,), sets=(r_A,))
        b.op("v", lambda e: e.tensor_tensor(A_t[:, SL], A_t[:, SL], ve[:, L], ALU.mult),
             reads=(r_A, m2), sets=(r_A,))
        b.op("v", lambda e: e.tensor_scalar(A_t[:, SL], A_t[:, SL], -0.5, 1.5,
                                            ALU.mult, ALU.add),
             reads=(r_A,), sets=(r_A,))
        b.op("v", lambda e: e.tensor_tensor(A_t[:, SL], A_t[:, SL], sc3[:, L], ALU.mult),
             reads=(r_A, m3), sets=(r_A,))
        b.op("v", lambda e: e.scalar_tensor_tensor(B_t[:, SL], mt[:, L], -1.0, A_t[:, SL],
                                                   ALU.mult, ALU.mult),
             reads=(m0, r_A), war=(r_B,), sets=(r_B,))

    for (qo, ql) in _chunks(0, S, 512):
        for c in range(NC_D):
            i = c % 2
            b.dma("y", xb[i][:, 0:ql], h_t[c * 128:(c + 1) * 128, qo:qo + ql], d_xb[i])
            b.op("s", lambda e, i=i, ql=ql: e.activation(st_sc[3][:, 0:ql],
                                                         xb[i][:, 0:ql], AF.Square),
                 reads=(d_xb[i],), war=(r_stsc[3],), sets=(r_stsc[3],))
            b.op("t", lambda e, i=i, ql=ql, c=c: e.matmul(
                psD[:, 0:ql], ao_f[:, :], xb[i][:, 0:ql],
                start=(c == 0), stop=(c == NC_D - 1)),
                reads=(d_xb[i], d_cst),
                war=((r_psD,) if c == 0 else ()),
                sets=((r_psD,) if c == NC_D - 1 else ()), tick=(c == NC_D - 1))
            b.op("t", lambda e, ql=ql, c=c: e.matmul(
                psE[:, 0:ql], ao_f[:, :], st_sc[3][:, 0:ql],
                start=(c == 0), stop=(c == NC_D - 1)),
                reads=(r_stsc[3],),
                war=((r_psE,) if c == 0 else ()),
                sets=((r_psE,) if c == NC_D - 1 else ()), tick=True)
        stats_chain(qo, ql, psD, r_psD, psE, r_psE)

    for c in range(NC_D):
        for (ho, hl) in _chunks(0, S, 1024):
            i = (c * (S // 1024) + ho // 1024) % 2
            b.dma("y", xb[i][:, 0:hl], h_t[c * 128:(c + 1) * 128, ho:ho + hl], d_xb[i])
            b.op("v", lambda e, i=i, ho=ho, hl=hl: e.tensor_tensor(
                tmpf[:, 0:hl], xb[i][:, 0:hl], A_t[:, ho:ho + hl], ALU.mult),
                reads=(d_xb[i], r_A), war=(r_tmpf,), sets=(r_tmpf,))
            b.op("v", lambda e, i=i, ho=ho, hl=hl: e.tensor_tensor(
                lnb[i][:, 0:hl], tmpf[:, 0:hl], B_t[:, ho:ho + hl], ALU.add),
                reads=(r_tmpf, r_B), war=(r_lnb[i],), sets=(r_lnb[i],))
            b.stream_write(ln1_d[c * 128:(c + 1) * 128, ho:ho + hl], lnb[i][:, 0:hl],
                           r_lnb[i])
    b.barrier_y(r_lnb)
    b.fence(stream_resses=(r_lnb,))

    def rope_combine(dst_ap, r_dst, src_ap, r_src, qo, ql):
        b.op("v", lambda e: e.tensor_tensor(rp1[:, 0:ql], src_ap, T1[:, qo:qo + ql],
                                            ALU.mult),
             reads=(r_src, d_cst), war=(r_rp1,), sets=(r_rp1,))
        b.op("v", lambda e: e.tensor_tensor(rp2[:, 0:ql], psD[:, 0:ql], T2[:, qo:qo + ql],
                                            ALU.mult),
             reads=(r_psD, d_cst), war=(r_rp2,), sets=(r_rp2,))
        b.op("v", lambda e: e.tensor_tensor(dst_ap, rp1[:, 0:ql], rp2[:, 0:ql], ALU.add),
             reads=(r_rp1, r_rp2), sets=(r_dst,))

    passes = [("dkv", wdkv, 0, NC_KV, BC_DKV)]
    for g0 in range(0, NC_D, 6):
        passes.append(("kr", wkr, g0, min(g0 + 6, NC_D), BC_KR))
    for g0 in range(0, NC_Q, 6):
        passes.append(("dq", wdq, g0, min(g0 + 6, NC_Q), BC_DQ))
    pshalf = [(psA, 0, r_psA[0]), (psA, 512, r_psA[1]), (psB, 0, r_psB[0]),
              (psB, 512, r_psB[1]), (psC, 0, r_psC[0]), (psC, 512, r_psC[1])]
    ibw = [0]
    for kind, wt, m0g, m1g, bc0 in passes:
        nm = m1g - m0g
        for (qo, ql) in QCS:
            for c in range(NC_D):
                iw = ibw[0] % 4
                ibw[0] += 1
                b.dma("g", wb_p2[iw][:, 0:nm * 128].rearrange("p (m f) -> p m f", m=nm),
                      wt[c, m0g:m1g, :, :].rearrange("m p f -> p m f"), d_wbp2[iw])
                b.dma("y", lrhs[iw][:, 0:ql], ln1_d[c * 128:(c + 1) * 128, qo:qo + ql],
                      d_lrhs[iw])
                for m in range(nm):
                    ps, po, rps = pshalf[m]
                    b.op("t", lambda e, ps=ps, po=po, ql=ql, iw=iw, m=m, c=c: e.matmul(
                        ps[:, po:po + ql], wb_p2[iw][:, m * 128:(m + 1) * 128],
                        lrhs[iw][:, 0:ql], start=(c == 0), stop=(c == NC_D - 1)),
                        reads=(d_wbp2[iw], d_lrhs[iw]),
                        war=((rps,) if c == 0 else ()),
                        sets=((rps,) if c == NC_D - 1 else ()),
                        tick=(m == nm - 1 or c == NC_D - 1))
            for m in range(nm):
                ps, po, rps = pshalf[m]
                ma = m0g + m
                if kind == "dkv":
                    b.op("s", lambda e, ps=ps, po=po, ql=ql, ma=ma, qo=qo: e.activation(
                        cKV[:, ma * S + qo:ma * S + qo + ql], ps[:, po:po + ql],
                        AF.Identity, bias=bp[:, BC_DKV + ma:BC_DKV + ma + 1]),
                        reads=(rps, d_cst), war=(r_cKV,), sets=(r_cKV,))
                elif kind == "dq":
                    i = m % 2
                    b.op("s", lambda e, ps=ps, po=po, ql=ql, ma=ma, i=i: e.activation(
                        cqst[i][:, 0:ql], ps[:, po:po + ql], AF.Identity,
                        bias=bp[:, BC_DQ + ma:BC_DQ + ma + 1]),
                        reads=(rps, d_cst), war=(r_cqst[i],), sets=(r_cqst[i],))
                    b.stream_write(cq_d[ma * 128:(ma + 1) * 128, qo:qo + ql],
                                   cqst[i][:, 0:ql], r_cqst[i])
                else:
                    i = m % 2
                    b.op("s", lambda e, ps=ps, po=po, ql=ql, ma=ma, i=i: e.activation(
                        krt[i][:, 0:ql], ps[:, po:po + ql], AF.Identity,
                        bias=bp[:, BC_KR + ma:BC_KR + ma + 1]),
                        reads=(rps, d_cst), war=(r_krt[i],), sets=(r_krt[i],))
                    b.op("t", lambda e, ql=ql, i=i: e.matmul(
                        psD[:, 0:ql], pi_b[:, :], krt[i][:, 0:ql], start=True, stop=True),
                        reads=(r_krt[i], d_cst), war=(r_psD,), sets=(r_psD,))
                    rope_combine(kR[:, ma * S + qo:ma * S + qo + ql], r_kR,
                                 krt[i][:, 0:ql], r_krt[i], qo, ql)
    b.barrier_y(r_cqst)
    b.fence(stream_resses=(r_cqst[0], r_cqst[1]))

    NQ = len(QCS)
    r_qCq = [R(f"qC{q}") for q in range(NQ)]
    r_qRq = [R(f"qR{q}") for q in range(NQ)]

    def rope_combine_e(dst_ap, r_dst, src_ap, r_src, qo, ql):
        b.op("v", lambda e: e.tensor_tensor(rp1[:, 0:ql], src_ap, T1[:, qo:qo + ql],
                                            ALU.mult),
             reads=(r_src, d_cst), war=(r_rp1,), sets=(r_rp1,))
        b.op("v", lambda e: e.tensor_tensor(rp2[:, 0:ql], psE[:, 0:ql], T2[:, qo:qo + ql],
                                            ALU.mult),
             reads=(r_psE, d_cst), war=(r_rp2,), sets=(r_rp2,))
        b.op("v", lambda e: e.tensor_tensor(dst_ap, rp1[:, 0:ql], rp2[:, 0:ql], ALU.add),
             reads=(r_rp1, r_rp2), sets=(r_dst,))

    def prod_dmas(h):
        iw = h % 2
        b.dma("g", wuq_b[iw][:, :].rearrange("p (k f) -> p k f", k=NC_Q),
              wuq[:, h, :, :].rearrange("k p f -> p k f"), d_wuq[iw])
        b.dma("g", wukv_b[iw][:, 0:NC_KV * 128].rearrange("p (k f) -> p k f", k=NC_KV),
              wuk[:, h, :, :].rearrange("k p f -> p k f"), d_wukv[iw])
        b.dma("g", wukv_b[iw][:, NC_KV * 128:].rearrange("p (k f) -> p k f", k=NC_KV),
              wuv[:, h, :, :].rearrange("k p f -> p k f"), d_wukv[iw])

    def prefetch_cqr(qi):
        qo, ql = QCS[qi]
        ic = qi % 2
        b.dma("y", cqr[ic][:, 0:NC_Q * ql].rearrange("p (k f) -> p k f", k=NC_Q),
              cq_d.rearrange("(k p) t -> p k t", p=128)[:, :, qo:qo + ql], d_cqr[ic])

    def prod_qc(h, qi):
        iw = h % 2
        qo, ql = QCS[qi]
        ic = qi % 2
        for kc in range(NC_Q):
            b.op("t", lambda e, kc=kc, ql=ql, ic=ic, iw=iw: e.matmul(
                psE[:, 0:ql], wuq_b[iw][:, kc * 128:(kc + 1) * 128],
                cqr[ic][:, kc * ql:kc * ql + ql],
                start=(kc == 0), stop=(kc == NC_Q - 1)),
                reads=(d_wuq[iw], d_cqr[ic]),
                war=((r_psE,) if kc == 0 else ()),
                sets=((r_psE,) if kc == NC_Q - 1 else ()), tick=(kc == NC_Q - 1))
        b.op("s", lambda e, qo=qo, ql=ql, h=h: e.activation(
            qC_h[:, qo:qo + ql], psE[:, 0:ql], AF.Identity,
            bias=bp[:, BC_UQ + h:BC_UQ + h + 1]),
            reads=(r_psE, d_cst), war=(r_qCq[qi],), sets=(r_qCq[qi],))
        b.op("t", lambda e, qo=qo, ql=ql, qi=qi: e.matmul(
            psE[:, 0:ql], pi_b[:, :], qC_h[:, qo:qo + ql], start=True, stop=True),
            reads=(r_qCq[qi], d_cst), war=(r_psE,), sets=(r_psE,))
        rope_combine_e(qR_h[:, qo:qo + ql], r_qRq[qi], qC_h[:, qo:qo + ql],
                       r_qCq[qi], qo, ql)

    def prod_kv(h):
        iw = h % 2
        for (qo, ql) in QCS:
            for kc in range(NC_KV):
                b.op("t", lambda e, kc=kc, qo=qo, ql=ql, iw=iw: e.matmul(
                    psE[:, 0:ql], wukv_b[iw][:, kc * 128:(kc + 1) * 128],
                    cKV[:, kc * S + qo:kc * S + qo + ql],
                    start=(kc == 0), stop=(kc == NC_KV - 1)),
                    reads=(d_wukv[iw], r_cKV),
                    war=((r_psE,) if kc == 0 else ()),
                    sets=((r_psE,) if kc == NC_KV - 1 else ()), tick=(kc == NC_KV - 1))
            b.op("s", lambda e, qo=qo, ql=ql, h=h: e.activation(
                kC_h[:, qo:qo + ql], psE[:, 0:ql], AF.Identity,
                bias=bp[:, BC_UK + h:BC_UK + h + 1]),
                reads=(r_psE, d_cst), war=(r_kC,), sets=(r_kC,))
        ntt = S // 128
        for g0 in range(0, ntt, 4):
            gn = min(4, ntt - g0)
            for tj in range(gn):
                tt = g0 + tj
                for kc in range(NC_KV):
                    b.op("t", lambda e, tt=tt, tj=tj, kc=kc, iw=iw: e.matmul(
                        psE[:, tj * 128:(tj + 1) * 128],
                        cKV[:, kc * S + tt * 128:kc * S + (tt + 1) * 128],
                        wukv_b[iw][:, (NC_KV + kc) * 128:(NC_KV + kc + 1) * 128],
                        start=(kc == 0), stop=(kc == NC_KV - 1)),
                        reads=(d_wukv[iw], r_cKV),
                        war=((r_psE,) if (kc == 0 and tj == 0) else ()),
                        sets=((r_psE,) if (kc == NC_KV - 1 and tj == gn - 1) else ()),
                        tick=(kc == NC_KV - 1 and tj == gn - 1))
            b.op("s", lambda e, g0=g0, gn=gn: e.activation(
                v_h[:, g0 * 128:(g0 + gn) * 128], psE[:, 0:gn * 128], AF.Identity),
                reads=(r_psE,), war=(r_v,), sets=(r_v,))

    sslot = [(psC, 0, r_psC[0]), (psC, 512, r_psC[1]),
             (psB, 0, r_psB[0]), (psB, 512, r_psB[1])]

    def attn_qc(h, qi):
        qo, ql = QCS[qi]
        nkb = (qo + ql - 1) // 128 + 1
        LAG = min(2, nkb)

        def emit_scores(kb):
            ie = kb % 4
            ps_s, pco, rsc = sslot[ie]
            b.op("t", lambda e, kb=kb, qo=qo, ql=ql, ps_s=ps_s, pco=pco, qi=qi: e.matmul(
                ps_s[:, pco:pco + ql], kC_h[:, kb * 128:(kb + 1) * 128],
                qC_h[:, qo:qo + ql], start=True, stop=False),
                reads=(r_kC, r_qCq[qi]), war=(rsc,), tick=False)
            b.op("t", lambda e, kb=kb, qo=qo, ql=ql, ps_s=ps_s, pco=pco, h=h, qi=qi:
                 e.matmul(ps_s[:, pco:pco + ql],
                          kR[:, h * S + kb * 128:h * S + (kb + 1) * 128],
                          qR_h[:, qo:qo + ql], start=False, stop=True),
                 reads=(r_kR, r_qRq[qi]), sets=(rsc,))
            b.op("s", lambda e, ps_s=ps_s, pco=pco, ql=ql, ie=ie: e.activation(
                expb[ie][:, 0:ql], ps_s[:, pco:pco + ql], AF.Exp, scale=SM_SCALE),
                reads=(rsc,), war=(r_expb[ie],), sets=(r_expb[ie],))
            dcol = kb * 128 - qo
            if dcol >= 0:
                if dcol > 0:
                    b.op("v", lambda e, ie=ie, dcol=dcol: e.memset(
                        expb[ie][:, 0:dcol], 0.0),
                        reads=(r_expb[ie],), sets=(r_expb[ie],))
                hi = min(dcol + 128, ql)
                b.op("v", lambda e, ie=ie, dcol=dcol, hi=hi: e.tensor_tensor(
                    expb[ie][:, dcol:hi], expb[ie][:, dcol:hi],
                    tri_b[:, 0:hi - dcol], ALU.mult),
                    reads=(r_expb[ie], d_cst), sets=(r_expb[ie],))

        def emit_consume(kb):
            ie = kb % 4
            b.op("t", lambda e, ie=ie, ql=ql, kb=kb, nkb=nkb: e.matmul(
                psD[:, 0:ql], ao_b[:, :], expb[ie][:, 0:ql],
                start=(kb == 0), stop=(kb == nkb - 1)),
                reads=(r_expb[ie], d_cst),
                war=((r_psD,) if kb == 0 else ()),
                sets=((r_psD,) if kb == nkb - 1 else ()), tick=(kb == nkb - 1))
            b.op("t", lambda e, ie=ie, ql=ql, kb=kb, nkb=nkb: e.matmul(
                psA[:, 0:ql], v_h[:, kb * 128:(kb + 1) * 128], expb[ie][:, 0:ql],
                start=(kb == 0), stop=(kb == nkb - 1)),
                reads=(r_v, r_expb[ie]),
                war=((r_psA[0],) if kb == 0 else ()),
                sets=((r_psA[0],) if kb == nkb - 1 else ()), tick=True)

        for kb in range(nkb + LAG):
            if kb < nkb:
                emit_scores(kb)
            if kb >= LAG:
                emit_consume(kb - LAG)
        ix = qi % 2
        b.op("v", lambda e, ql=ql: e.reciprocal(rcp[:, 0:ql], psD[:, 0:ql]),
             reads=(r_psD,), war=(r_rcp,), sets=(r_rcp,))
        b.op("v", lambda e, ql=ql, ix=ix: e.tensor_tensor(
            cxst[ix][:, 0:ql], psA[:, 0:ql], rcp[:, 0:ql], ALU.mult),
            reads=(r_psA[0], r_rcp), war=(r_cxst[ix],), sets=(r_cxst[ix],))
        b.op("v", lambda e, ql=ql, ix=ix, h=h: e.tensor_scalar(
            cxst[ix][:, 0:ql], cxst[ix][:, 0:ql],
            bp[:, BC_UV + h:BC_UV + h + 1], None, ALU.add),
            reads=(r_cxst[ix], d_cst), sets=(r_cxst[ix],))
        b.stream_write(ctx_d[h * 128:(h + 1) * 128, qo:qo + ql],
                       cxst[ix][:, 0:ql], r_cxst[ix])

    prod_dmas(0)
    for qi in range(NQ):
        prefetch_cqr(qi)
        prod_qc(0, qi)
    prod_kv(0)
    for h in range(NH):
        if h + 1 < NH:
            prod_dmas(h + 1)
        for qi in range(NQ):
            if h + 1 < NH:
                prefetch_cqr(qi)
            attn_qc(h, qi)
            if h + 1 < NH:
                prod_qc(h + 1, qi)
        if h + 1 < NH:
            prod_kv(h + 1)
    b.barrier_y(r_cxst)
    b.fence(stream_resses=(r_cxst[0], r_cxst[1]))

    pj = [(psA, 0, r_psA[0]), (psA, 512, r_psA[1]), (psB, 0, r_psB[0]),
          (psB, 512, r_psB[1])]
    for (qo, ql) in QCS:
        cidx = [0]
        for jg in range(NC_D // 4):
            for hh in range(2):
                b.dma("g", wo_b[hh][:, :].rearrange("p (kj f) -> p kj f", kj=32),
                      wo[jg, hh * 8:(hh + 1) * 8, :, :, :]
                      .rearrange("k j p f -> p (k j) f"), d_wo[hh])
            for hd in range(NC_D):
                icx = hd % 4
                b.dma("y", cxbuf[icx][:, 0:ql],
                      ctx_d[hd * 128:(hd + 1) * 128, qo:qo + ql], d_cx[icx])
                hh, hdo = hd // 8, hd % 8
                for j in range(4):
                    ps, po, rps = pj[j]
                    b.op("t", lambda e, ps=ps, po=po, ql=ql, hh=hh, hdo=hdo, icx=icx,
                         j=j, hd=hd: e.matmul(
                             ps[:, po:po + ql],
                             wo_b[hh][:, (hdo * 4 + j) * 128:(hdo * 4 + j + 1) * 128],
                             cxbuf[icx][:, 0:ql],
                             start=(hd == 0), stop=(hd == NC_D - 1)),
                        reads=(d_wo[hh], d_cx[icx]),
                        war=((rps,) if hd == 0 else ()),
                        sets=((rps,) if hd == NC_D - 1 else ()),
                        tick=(j == 3 or hd == NC_D - 1))
            for j in range(4):
                ps, po, rps = pj[j]
                jj = jg * 4 + j
                ih = jj % 2
                b.dma("y", hbuf[ih][:, 0:ql], h_t[jj * 128:(jj + 1) * 128, qo:qo + ql],
                      d_hb[ih])
                if qo < HALF:
                    dst = x1[:, jj * HALF + qo:jj * HALF + qo + ql]
                    rdst = r_x1
                else:
                    ixs = jj % 2
                    dst = x1st[ixs][:, 0:ql]
                    rdst = r_x1st[ixs]
                b.op("v", lambda e, ps=ps, po=po, ql=ql, jj=jj, ih=ih, dst=dst:
                     e.scalar_tensor_tensor(dst, ps[:, po:po + ql],
                                            bp[:, BC_O + jj:BC_O + jj + 1],
                                            hbuf[ih][:, 0:ql], ALU.add, ALU.add),
                     reads=(rps, d_hb[ih], d_cst), war=(rdst,), sets=(rdst,))
                if qo >= HALF:
                    ixs = jj % 2
                    b.stream_write(
                        x1b_d[jj * 128:(jj + 1) * 128, qo - HALF:qo - HALF + ql],
                        x1st[ixs][:, 0:ql], r_x1st[ixs])
                    sfs, r_sfs = x1st[ixs][:, 0:ql], r_x1st[ixs]
                else:
                    sfs, r_sfs = dst, r_x1
                ci = cidx[0]
                cidx[0] += 1
                b.op("s", lambda e, sfs=sfs, ql=ql: e.activation(
                    xsq[:, 0:ql], sfs, AF.Square),
                    reads=(r_sfs,), war=(r_xsq,), sets=(r_xsq,))
                b.op("t", lambda e, sfs=sfs, ql=ql, ci=ci: e.matmul(
                    psC[:, 0:ql], ao_f[:, :], sfs,
                    start=(ci == 0), stop=(ci == NC_D - 1)),
                    reads=(r_sfs, d_cst),
                    war=((r_psC[0],) if ci == 0 else ()),
                    sets=((r_psC[0],) if ci == NC_D - 1 else ()), tick=(ci == NC_D - 1))
                b.op("t", lambda e, ql=ql, ci=ci: e.matmul(
                    psC[:, 512:512 + ql], ao_f[:, :], xsq[:, 0:ql],
                    start=(ci == 0), stop=(ci == NC_D - 1)),
                    reads=(r_xsq,),
                    war=((r_psC[1],) if ci == 0 else ()),
                    sets=((r_psC[1],) if ci == NC_D - 1 else ()), tick=True)
        stats_chain(qo, ql, psC, r_psC[0], psC[:, 512:1024], r_psC[1])

    def ln2_gate(t0, chunk_dep=None):
        for c in range(NC_D):
            dep0 = (r_A,) if chunk_dep else (r_x1, r_A)
            for (ho, hl) in _chunks(0, HALF, 1024):
                if chunk_dep:
                    b._dep_wait("v", [chunk_dep[c]], [])
                b.op("v", lambda e, c=c, ho=ho, hl=hl: e.tensor_tensor(
                    tmpf[:, 0:hl], x1[:, c * HALF + ho:c * HALF + ho + hl],
                    A_t[:, t0 + ho:t0 + ho + hl], ALU.mult),
                    reads=dep0, war=(r_tmpf,), sets=(r_tmpf,))
                b.op("v", lambda e, c=c, ho=ho, hl=hl: e.tensor_tensor(
                    ln2[:, c * HALF + ho:c * HALF + ho + hl], tmpf[:, 0:hl],
                    B_t[:, t0 + ho:t0 + ho + hl], ALU.add),
                    reads=(r_tmpf, r_B), war=(r_ln2c[c],), sets=(r_ln2c[c],))
        glog, sm, swk, msk, mx1, rs1 = g_sc
        g0, g1, g2, g3, g4, g5 = r_gsc
        NT = HALF // 128
        for tt in range(NT):
            for c in range(NC_D):
                b.op("t", lambda e, tt=tt, c=c: e.matmul(
                    psD[:, 0:16], ln2[:, c * HALF + tt * 128:c * HALF + (tt + 1) * 128],
                    gw_b[:, c * 16:(c + 1) * 16],
                    start=(c == 0), stop=(c == NC_D - 1)),
                    reads=(r_ln2c[c], d_gwc),
                    war=((r_psD,) if c == 0 else ()),
                    sets=((r_psD,) if c == NC_D - 1 else ()), tick=(c == NC_D - 1))
            b.op("v", lambda e: e.reduce_max(mx1[:, 0:1], psD[:, 0:16], axis=AX.X,
                                             negate=True),
                 reads=(r_psD,), war=(g4,), sets=(g4,))
            b.op("s", lambda e: e.activation(glog[:, :], psD[:, 0:16], AF.Exp,
                                             bias=mx1[:, 0:1]),
                 reads=(r_psD, g4), war=(g0,), sets=(g0,))
            b.op("v", lambda e: e.reduce_sum(rs1[:, 0:1], glog[:, :], axis=AX.X),
                 reads=(g0,), war=(g5,), sets=(g5,))
            b.op("v", lambda e: e.reciprocal(rs1[:, 0:1], rs1[:, 0:1]),
                 reads=(g5,), sets=(g5,))
            b.op("v", lambda e: e.tensor_scalar(sm[:, :], glog[:, :], rs1[:, 0:1], None,
                                                ALU.mult),
                 reads=(g0, g5), war=(g1,), sets=(g1,))
            b.op("v", lambda e: e.tensor_copy(swk[:, :], sm[:, :]),
                 reads=(g1,), war=(g2,), sets=(g2,))
            for it in range(6):
                b.op("v", lambda e: e.reduce_max(mx1[:, 0:1], swk[:, :], axis=AX.X),
                     reads=(g2,), war=(g4,), sets=(g4,))
                if it < 5:
                    b.op("v", lambda e: e.tensor_scalar(mski[:, :], swk[:, :], mx1[:, 0:1],
                                                        None, ALU.is_ge),
                         reads=(g2, g4), war=(g3,), sets=(g3,))
                    b.op("v", lambda e: e.select(swk[:, :], mski[:, :], negt[:, :],
                                                 swk[:, :], add_drain=True),
                         reads=(g3, r_cst, g2), sets=(g2,))
            b.op("v", lambda e: e.tensor_scalar(msk[:, :], sm[:, :], mx1[:, 0:1], None,
                                                ALU.is_ge),
                 reads=(g1, g4), war=(g3,), sets=(g3,))
            b.op("v", lambda e, tt=tt: e.tensor_tensor(
                swkp[:, tt * 16:(tt + 1) * 16], sm[:, :], msk[:, :], ALU.mult),
                reads=(g1, g3), war=(r_swkp[tt],), sets=(r_swkp[tt],))
        for tt in range(NT):
            b.op("t", lambda e, tt=tt: e.transpose(
                psE[0:16, 0:128], swkp[:, tt * 16:(tt + 1) * 16], idn[:, :]),
                reads=(r_swkp[tt], d_cst), war=(r_psE,), sets=(r_psE,))
            b.op("v", lambda e, tt=tt: e.tensor_copy(
                cwT[0:16, t0 + tt * 128:t0 + (tt + 1) * 128], psE[0:16, 0:128]),
                reads=(r_psE,), war=(r_cwT,), sets=(r_cwT,))

    def ffn_half(t0):
        TCH = _chunks(0, HALF, 512)
        for ex in range(E):
            for ci, (co, cl) in enumerate(TCH):
                ps, rps = (psD, r_psD) if ci % 2 == 0 else (psE, r_psE)
                b.op("t", lambda e, ps=ps, co=co, cl=cl, ex=ex: e.matmul(
                    ps[:, 0:cl], selm[0:16, ex * 128:(ex + 1) * 128],
                    cwT[0:16, t0 + co:t0 + co + cl],
                    start=True, stop=True),
                    reads=(r_cwT, d_cst), war=(rps,), sets=(rps,))
                b.op("s", lambda e, ps=ps, co=co, cl=cl: e.activation(
                    cwb[:, co:co + cl], ps[:, 0:cl], AF.Identity),
                    reads=(rps,), war=(r_cwb,), sets=(r_cwb,))
            for m in range(NC_M):
                img = (ex * NC_M + m) % 2
                b.dma("g", wgb[img][:, :].rearrange("p (k f) -> p k f", k=NC_D),
                      wg[ex, m, :, :, :].rearrange("k p f -> p k f"), d_wgb[img])
                b.dma("g", wub[img][:, :].rearrange("p (k f) -> p k f", k=NC_D),
                      wu[ex, m, :, :, :].rearrange("k p f -> p k f"), d_wub[img])
                for k in range(NC_D):
                    for ci, (co, cl) in enumerate(TCH):
                        b.op("t", lambda e, k=k, co=co, cl=cl, ci=ci, img=img: e.matmul(
                            psA[:, ci * 512:ci * 512 + cl],
                            wgb[img][:, k * 128:(k + 1) * 128],
                            ln2[:, k * HALF + co:k * HALF + co + cl],
                            start=(k == 0), stop=(k == NC_D - 1)),
                            reads=(d_wgb[img], r_ln2c[k]),
                            war=((r_psA[ci],) if k == 0 else ()),
                            sets=((r_psA[ci],) if k == NC_D - 1 else ()),
                            tick=(k == NC_D - 1))
                for k in range(NC_D):
                    for ci, (co, cl) in enumerate(TCH):
                        b.op("t", lambda e, k=k, co=co, cl=cl, ci=ci, img=img: e.matmul(
                            psB[:, ci * 512:ci * 512 + cl],
                            wub[img][:, k * 128:(k + 1) * 128],
                            ln2[:, k * HALF + co:k * HALF + co + cl],
                            start=(k == 0), stop=(k == NC_D - 1)),
                            reads=(d_wub[img], r_ln2c[k]),
                            war=((r_psB[ci],) if k == 0 else ()),
                            sets=((r_psB[ci],) if k == NC_D - 1 else ()),
                            tick=(k == NC_D - 1))
                rA = (r_psA[0], r_psA[1]) if len(TCH) > 1 else (r_psA[0],)
                rB = (r_psB[0], r_psB[1]) if len(TCH) > 1 else (r_psB[0],)
                b.op("s", lambda e: e.activation(gs_t[:, 0:HALF], psA[:, 0:HALF],
                                                 AF.Silu),
                     reads=rA, war=(r_gs,), sets=(r_gs,))
                b.op("v", lambda e: e.scalar_tensor_tensor(
                    us_t[:, 0:HALF], psB[:, 0:HALF], 1.0, cwb[:, 0:HALF],
                    ALU.mult, ALU.mult),
                    reads=rB + (r_cwb,), war=(r_us,), sets=(r_us,))
                b.op("v", lambda e, m=m: e.tensor_tensor(
                    t_e[:, m * HALF:(m + 1) * HALF], gs_t[:, 0:HALF], us_t[:, 0:HALF],
                    ALU.mult),
                    reads=(r_gs, r_us), war=(r_te,), sets=(r_te,))
            for j in range(NC_D):
                ij = (ex * NC_D + j) % 2
                b.dma("g", wdb[ij][:, :].rearrange("p (m f) -> p m f", m=NC_M),
                      wd[ex, j, :, :, :].rearrange("m p f -> p m f"), d_wdb[ij])
                for m in range(NC_M):
                    for ci, (co, cl) in enumerate(TCH):
                        b.op("t", lambda e, m=m, co=co, cl=cl, ci=ci, ij=ij: e.matmul(
                            psC[:, ci * 512:ci * 512 + cl],
                            wdb[ij][:, m * 128:(m + 1) * 128],
                            t_e[:, m * HALF + co:m * HALF + co + cl],
                            start=(m == 0), stop=(m == NC_M - 1)),
                            reads=(d_wdb[ij], r_te),
                            war=((r_psC[ci],) if m == 0 else ()),
                            sets=((r_psC[ci],) if m == NC_M - 1 else ()),
                            tick=(m == NC_M - 1))
                for ci, (co, cl) in enumerate(TCH):
                    b.op("v", lambda e, j=j, co=co, cl=cl, ci=ci: e.tensor_tensor(
                        x1[:, j * HALF + co:j * HALF + co + cl],
                        x1[:, j * HALF + co:j * HALF + co + cl],
                        psC[:, ci * 512:ci * 512 + cl], ALU.add),
                        reads=(r_psC[ci], r_x1), sets=(r_x1,))
        out_writes.append(lambda c, t0=t0: b.stream_write(
            out_t[c * 128:(c + 1) * 128, t0:t0 + HALF],
            x1[:, c * HALF:(c + 1) * HALF], r_x1))

    out_writes = []
    b.fence(stream_resses=(r_x1st[0], r_x1st[1]))
    ln2_gate(0)
    ffn_half(0)
    b.barrier_y(r_x1st)
    reload_stamp = []
    for c in range(NC_D):
        out_writes[0](c)
        i = c % 2
        b._dep_wait("y", [("w", r_x1.wsem, r_x1.wcnt)], [])
        b.dma("y", x1[:, c * HALF:(c + 1) * HALF], x1b_d[c * 128:(c + 1) * 128, :],
              d_x1r[i])
        reload_stamp.append(("y", d_x1r[i].dmasem, d_x1r[i].dmacnt))
    out_writes.clear()
    r_x1.producer = reload_stamp[-1]
    r_x1.readers = list(reload_stamp)
    ln2_gate(HALF, chunk_dep=reload_stamp)
    ffn_half(HALF)
    for c in range(NC_D):
        out_writes[0](c)
    b.barrier_y([r_x1])

    with nc.Block() as block:
        @block.tensor
        def _(e):
            for f in b.q["t"]:
                f(e)

        @block.scalar
        def _(e):
            for f in b.q["s"]:
                f(e)

        @block.vector
        def _(e):
            for f in b.q["v"]:
                f(e)

        @block.gpsimd
        def _(e):
            for f in b.q["g"]:
                f(e)

        @block.sync
        def _(e):
            for f in b.q["y"]:
                f(e)

    b.close_sems()
    return nc



def _eo_perm():
    p = []
    for hh in range(NH):
        base = hh * HDIM
        p += list(range(base, base + HDIM, 2)) + list(range(base + 1, base + HDIM, 2))
    return np.array(p)


def _tile_km(w, nk, nm):
    return np.ascontiguousarray(w.reshape(nk, 128, nm, 128).transpose(0, 2, 1, 3))


def host_prep(ins, S):
    perm = _eo_perm()
    d = {}
    d["wdkv"] = _tile_km(ins["L0_attn_W_DKV"], NC_D, NC_KV)
    d["wuk"] = _tile_km(ins["L0_attn_W_UK"][:, perm], NC_KV, NH)
    d["wuv"] = _tile_km(ins["L0_attn_W_UV"], NC_KV, NH)
    d["wdq"] = _tile_km(ins["L0_attn_W_DQ"], NC_D, NC_Q)
    d["wuq"] = _tile_km(ins["L0_attn_W_UQ"][:, perm], NC_Q, NH)
    d["wkr"] = _tile_km(ins["L0_attn_W_KR"][:, perm], NC_D, NC_D)
    wot = _tile_km(ins["L0_attn_W_O"], NC_D, NC_D)
    d["wo"] = np.ascontiguousarray(
        wot.reshape(NC_D, NC_D // 4, 4, 128, 128).transpose(1, 0, 2, 3, 4))
    d["gw"] = np.ascontiguousarray(ins["L0_ffn_gate_w"].T.reshape(NC_D, 128, 16))
    wgp = ins["L0_ffn_w_gate"].transpose(0, 2, 1)
    wup = ins["L0_ffn_w_up"].transpose(0, 2, 1)
    wdp = ins["L0_ffn_w_down"].transpose(0, 2, 1)
    d["wg"] = np.ascontiguousarray(
        wgp.reshape(E, NC_D, 128, NC_M, 128).transpose(0, 3, 1, 2, 4))
    d["wu"] = np.ascontiguousarray(
        wup.reshape(E, NC_D, 128, NC_M, 128).transpose(0, 3, 1, 2, 4))
    d["wd"] = np.ascontiguousarray(
        wdp.reshape(E, NC_M, 128, NC_D, 128).transpose(0, 3, 1, 2, 4))
    bpk = np.zeros((128, 98), dtype=np.float32)
    bpk[:, 97] = 1.0

    def put(col, vec):
        n = len(vec) // 128
        bpk[:, col:col + n] = vec.reshape(n, 128).T
    put(0, ins["L0_attn_b_DKV"])
    put(4, ins["L0_attn_b_UK"][perm])
    put(20, ins["L0_attn_b_UV"])
    put(36, ins["L0_attn_b_DQ"])
    put(48, ins["L0_attn_b_UQ"][perm])
    put(64, ins["L0_attn_b_KR"][perm])
    put(80, ins["L0_attn_b_O"])
    d["biasp"] = bpk
    pos = np.arange(S, dtype=np.float64)
    invf = 1.0 / (10000.0 ** (np.arange(64, dtype=np.float64) * 2.0 / 128))
    ang = invf[:, None] * pos[None, :]
    cosd, sind = np.cos(ang), np.sin(ang)
    d["t1d"] = np.concatenate([cosd, cosd], axis=0).astype(np.float32)
    d["t2d"] = np.concatenate([-sind, sind], axis=0).astype(np.float32)
    d["cst_ao_f"] = np.ones((128, 128), dtype=np.float32)
    d["cst_ao_b"] = np.ones((128, 128), dtype=BF)
    d["cst_id"] = np.eye(128, dtype=np.float32)
    pim = np.zeros((128, 128), dtype=np.float32)
    for m in range(128):
        pim[(m + 64) % 128, m] = 1.0
    d["cst_pi"] = pim.astype(BF)
    tri = (np.arange(128)[:, None] <= np.arange(128)[None, :]).astype(np.float32)
    d["cst_tri"] = tri.astype(BF)
    d["cst_o1b"] = np.ones((1, 128), dtype=BF)
    sel = np.zeros((16, 16 * 128), dtype=np.float32)
    for ex in range(16):
        sel[ex, ex * 128:(ex + 1) * 128] = 1.0
    d["cst_sel"] = sel.astype(BF)
    return d


def kernel(**inputs):
    S = 2048
    ncore = 8
    shared = host_prep(inputs, S)
    h = inputs["L0_attn_h"]
    nc = build_program(S)
    in_maps = []
    for c in range(ncore):
        m = dict(shared)
        m["h_t"] = np.ascontiguousarray(h[c].T)
        in_maps.append(m)
    res = run_bass_kernel_spmd(nc, in_maps, core_ids=list(range(ncore)))
    out = np.empty((ncore, S, D), dtype=np.float32)
    for c in range(ncore):
        out[c] = res.results[c]["out_t"].T
    return out
